# version 03
# baseline (speedup 1.0000x reference)
import contextlib
import numpy as np
import ml_dtypes
import concourse.bass as bass
import concourse.mybir as mybir
from concourse.bass_utils import run_bass_kernel_spmd

F32 = mybir.dt.float32
BF16 = mybir.dt.bfloat16
I32 = mybir.dt.int32
AF = mybir.ActivationFunctionType
ALU = mybir.AluOpType
AX = mybir.AxisListType

RNE = 3 * 2.0 ** 22
NT = 16
T = 2048
D = 1024

ENGINES = ("sync", "gpsimd", "vector", "scalar", "tensor")

ALL_BLOCKS = ("l0a", "l0f", "l1a", "l1f", "l2a", "l2f", "l3a", "l3f")


class K:

    def __init__(self, nc):
        self.nc = nc
        self.q = {e: [] for e in ENGINES}
        self.cnt = {e: 0 for e in ENGINES}
        self.hwm = {e: {f: 0 for f in ENGINES} for e in ENGINES}

    def emit(self, eng, fn):
        self.q[eng].append(("op", fn))
        return None

    def sig(self, eng, fn, n=1):
        self.cnt[eng] += n
        self.q[eng].append(("sig", fn, n))
        return (eng, self.cnt[eng])

    def dma(self, eng, out, in_):
        return self.sig(eng, lambda e, o=out, i=in_: e.dma_start(o, i), 16)

    def wait(self, eng, tok):
        if tok is None:
            return
        src, val = tok
        if src == eng:
            return
        if self.hwm[eng][src] >= val:
            return
        self.hwm[eng][src] = val
        self.q[eng].append(("wait", src, val))

    def wait_self(self, eng):
        self.q[eng].append(("wait", eng, self.cnt[eng]))

    def barrier(self):
        snap = dict(self.cnt)
        for e in ENGINES:
            for f in ENGINES:
                if e != f:
                    self.wait(e, (f, snap[f]))

    def tok(self, eng):
        return (eng, self.cnt[eng])

    def check_deadlock(self):
        pos = {e: 0 for e in ENGINES}
        done = {e: 0 for e in ENGINES}
        progress = True
        while progress:
            progress = False
            for e in ENGINES:
                q = self.q[e]
                while pos[e] < len(q):
                    it = q[pos[e]]
                    if it[0] == "wait" and done[it[1]] < it[2]:
                        break
                    if it[0] == "sig":
                        done[e] += it[2]
                    pos[e] += 1
                    progress = True
        stuck = {e: (pos[e], len(self.q[e])) for e in ENGINES if pos[e] < len(self.q[e])}
        if stuck:
            for e, (p, n) in stuck.items():
                it = self.q[e][p]
                print(f"DEADLOCK {e} at {p}/{n} waiting {it[1]} >= {it[2]} (has {done[it[1]]})")
            return False
        return True

    def realize(self):
        nc = self.nc
        sems = {}
        stack = contextlib.ExitStack()
        for e in ENGINES:
            sems[e] = stack.enter_context(nc.semaphore(f"sem_{e}"))
        with stack:
            with nc.Block() as block:
                def make(engine_name):
                    def body(eng):
                        for item in self.q[engine_name]:
                            if item[0] == "op":
                                item[1](eng)
                            elif item[0] == "sig":
                                item[1](eng).then_inc(sems[engine_name], item[2])
                            else:
                                eng.wait_ge(sems[item[1]], item[2])
                    return body
                block.sync(make("sync"))
                block.gpsimd(make("gpsimd"))
                block.vector(make("vector"))
                block.scalar(make("scalar"))
                block.tensor(make("tensor"))


class Banks:

    def __init__(self, k):
        self.k = k
        self.w = [None] * 8
        self.r = [[] for _ in range(8)]

    def pre_mm(self, b):
        for tok in self.r[b]:
            self.k.wait("tensor", tok)
        self.r[b] = []

    def set_w(self, b, tok):
        self.w[b] = tok

    def pre_read(self, b, eng):
        self.k.wait(eng, self.w[b])

    def add_r(self, b, tok):
        self.r[b].append(tok)



def _prep_weights(i):
    w = {}
    w["l0_wqkv"] = i["L0_attn_w_qkv"].reshape(1024, 3072)
    w["l0_bqkv"] = i["L0_attn_b_qkv"].reshape(1, 3072)
    w["l0_wout"] = i["L0_attn_w_out"].reshape(1024, 1024)
    w["l0_bout"] = i["L0_attn_b_out"].reshape(1, 1024)
    w["l0_w1t"] = i["L0_ffn_w1"].T
    w["l0_convw"] = i["L0_ffn_conv_w"].reshape(4096, 3).T.reshape(3, 32, 128).transpose(2, 0, 1).reshape(128, 96)
    w["l0_convb"] = i["L0_ffn_conv_b"].reshape(32, 128).T
    w["l0_w2t"] = i["L0_ffn_w2"].T
    w["l1_lnw"] = np.tile(i["L1_attn_ln_w"][None, :], (128, 1))
    w["l1_lnb"] = np.tile(i["L1_attn_ln_b"][None, :], (128, 1))
    w["l1_wqkvt"] = i["L1_attn_w_qkv"].T
    w["l1_woutt"] = i["L1_attn_w_out"].T
    w["l1_wt"] = i["L1_ffn_weight"].T
    for nm in ("Wq", "Wk", "Wv", "Wo", "Wg1", "Wg2"):
        w[f"l2_{nm.lower()}"] = i[f"L2_attn_{nm}"]
    w["l2_gamma"] = np.tile(i["L2_attn_gamma"][None, :], (128, 1))
    w["l2_beta"] = np.tile(i["L2_attn_beta"][None, :], (128, 1))
    for nm in ("theta", "phi", "theta_noise", "phi_noise"):
        w[f"l2_{nm}"] = i[f"L2_ffn_{nm}"]
    for kk in range(1, 5):
        w[f"l3_w{kk}"] = i[f"L3_attn_W{kk}"]
        w[f"l3_b{kk}"] = i[f"L3_attn_b{kk}"].reshape(1, 1024)
    w["l3_wgt"] = i["L3_ffn_w_gate"].T
    w["l3_wut"] = i["L3_ffn_w_up"].T
    w["l3_wdt"] = i["L3_ffn_w_down"].T
    out = {}
    for kk, v in w.items():
        v = np.asarray(v)
        out[kk] = np.ascontiguousarray(v)
    return out


def _consts():
    c = {}
    c["identb"] = np.eye(128, dtype=ml_dtypes.bfloat16)
    c["identf"] = np.eye(128, dtype=np.float32)
    c["onesb"] = np.ones((128, 128), dtype=ml_dtypes.bfloat16)
    c["onesf"] = np.ones((128, 128), dtype=np.float32)
    r = np.arange(128)
    diag = (r[:, None] <= r[None, :]).astype(np.float32)
    mg = np.zeros((128, 4, 512), dtype=ml_dtypes.bfloat16)
    for j in range(4):
        for qt in range(4):
            blk = np.ones((128, 128), np.float32) if j < qt else (diag if j == qt else np.zeros((128, 128), np.float32))
            mg[:, j, qt * 128:(qt + 1) * 128] = blk.astype(ml_dtypes.bfloat16)
    c["l0mask"] = mg.reshape(128, 2048)
    d_diag = r[None, :] - r[:, None]
    m_diag = ((d_diag >= 0) & (d_diag < 32)).astype(np.float32)
    d_prev = r[None, :] + 128 - r[:, None]
    m_prev = ((d_prev >= 0) & (d_prev < 32)).astype(np.float32)
    c["l1maskd"] = m_diag.astype(ml_dtypes.bfloat16)
    c["l1maskp"] = m_prev.astype(ml_dtypes.bfloat16)
    c["l2triu"] = (r[:, None] <= r[None, :]).astype(ml_dtypes.bfloat16)
    inv1 = 1.0 / 10000.0 ** (np.arange(0, 64, 2, dtype=np.float64) / 64)
    tpos = np.arange(T, dtype=np.float64)
    fr = tpos[:, None] * inv1[None, :]
    fr = np.concatenate([fr, fr], axis=1)
    cos1 = np.tile(np.cos(fr), (1, 16))
    sin1 = np.sin(fr)
    sin1a = np.concatenate([-sin1[:, 32:64], sin1[:, 0:32]], axis=1)
    c["l1cos"] = cos1.astype(ml_dtypes.bfloat16)
    c["l1sina"] = np.tile(sin1a, (1, 16)).astype(ml_dtypes.bfloat16)
    inv2 = 10000.0 ** (-(np.arange(32, dtype=np.float64) * 2.0) / 64)
    ang = tpos[:, None] * inv2[None, :]
    cos2 = np.tile(np.concatenate([np.cos(ang)] * 2, axis=1), (1, 16))
    sin2a = np.concatenate([-np.sin(ang), np.sin(ang)], axis=1)
    c["l2cos"] = cos2.astype(ml_dtypes.bfloat16)
    c["l2sina"] = np.tile(sin2a, (1, 16)).astype(ml_dtypes.bfloat16)
    return {kk: np.ascontiguousarray(v) for kk, v in c.items()}


HOST_INPUTS = {
    "l0_wqkv": ([1024, 3072], F32), "l0_bqkv": ([1, 3072], F32),
    "l0_wout": ([1024, 1024], F32), "l0_bout": ([1, 1024], F32),
    "l0_w1t": ([1024, 4096], F32), "l0_convw": ([128, 96], F32),
    "l0_convb": ([128, 32], F32), "l0_w2t": ([4096, 1024], F32),
    "l1_lnw": ([128, 1024], F32), "l1_lnb": ([128, 1024], F32),
    "l1_wqkvt": ([1024, 3072], F32), "l1_woutt": ([1024, 1024], F32),
    "l1_wt": ([1024, 1024], F32),
    "l2_wq": ([1024, 1024], F32), "l2_wk": ([1024, 1024], F32),
    "l2_wv": ([1024, 1024], F32), "l2_wo": ([1024, 1024], F32),
    "l2_wg1": ([1024, 64], F32), "l2_wg2": ([64, 1024], F32),
    "l2_gamma": ([128, 1024], F32), "l2_beta": ([128, 1024], F32),
    "l2_theta": ([1024, 1024], F32), "l2_phi": ([1024, 1024], F32),
    "l2_theta_noise": ([1024, 1024], F32), "l2_phi_noise": ([1024, 1024], F32),
    "l3_w1": ([1024, 1024], F32), "l3_b1": ([1, 1024], F32),
    "l3_w2": ([1024, 1024], F32), "l3_b2": ([1, 1024], F32),
    "l3_w3": ([1024, 1024], F32), "l3_b3": ([1, 1024], F32),
    "l3_w4": ([1024, 1024], F32), "l3_b4": ([1, 1024], F32),
    "l3_wgt": ([1024, 3072], I32), "l3_wut": ([1024, 3072], I32),
    "l3_wdt": ([3072, 1024], I32),
    "identb": ([128, 128], BF16), "identf": ([128, 128], F32),
    "onesb": ([128, 128], BF16), "onesf": ([128, 128], F32),
    "l0mask": ([128, 2048], BF16),
    "l1maskd": ([128, 128], BF16), "l1maskp": ([128, 128], BF16),
    "l2triu": ([128, 128], BF16),
    "l1cos": ([T, 1024], BF16), "l1sina": ([T, 1024], BF16),
    "l2cos": ([T, 1024], BF16), "l2sina": ([T, 1024], BF16),
}



class MK:
    def __init__(self, blocks=ALL_BLOCKS):
        self.blocks = blocks
        nc = bass.Bass("TRN2", target_bir_lowering=False)
        self.nc = nc
        self.dram = {"x": nc.dram_tensor("x", [T, D], F32, kind="ExternalInput")}
        for nm, (sh, dt) in HOST_INPUTS.items():
            self.dram[nm] = nc.dram_tensor(nm, sh, dt, kind="ExternalInput")
        self.out = nc.dram_tensor("out", [T, D], F32, kind="ExternalOutput")
        self.d_w1q = nc.dram_tensor("w1q", [1024, 4096], BF16, kind="Internal")
        self.d_w2q = nc.dram_tensor("w2q", [4096, 1024], BF16, kind="Internal")
        self.d_rows = nc.dram_tensor("rowscratch", [16, 128], BF16, kind="Internal")

        self.ctx = contextlib.ExitStack()

    def build(self):
        nc, ctx = self.nc, self.ctx

        def sb(name, parts, cols, dt=F32):
            return ctx.enter_context(nc.sbuf_tensor(name, [parts, cols], dt))

        with ctx:
            self.X = sb("X", 128, NT * D)
            self.SCRA = sb("SCRA", 128, 1024)
            self.SCRB = sb("SCRB", 128, 1024)
            self.SCRC = sb("SCRC", 128, 1024)
            self.SCRD = sb("SCRD", 128, 512)
            self.STAT = sb("STAT", 128, 192)
            self.ROWS = sb("ROWS", 128, 512)
            self.IDB = sb("IDB", 128, 128, BF16)
            self.IDF = sb("IDF", 128, 128, F32)
            self.ONB = sb("ONB", 128, 128, BF16)
            self.ONF = sb("ONF", 128, 128, F32)
            self.MKB = sb("MKB", 128, 2048 + 128 * 3, BF16)
            self.ROWSB = sb("ROWSB", 128, 4096, BF16)
            self.STG = sb("STG", 16, 128, BF16)
            self.CWT = sb("CWT", 128, 96)
            self.CBT = sb("CBT", 128, 32)
            self.ZROWS = sb("ZROWS", 128, 2048, BF16)
            self.AR = sb("AR", 128, 55296, BF16)
            self.PS = ctx.enter_context(nc.psum_tensor("PS", [128, 4096], F32))

            self.k = K(nc)
            self.bt = Banks(self.k)
            self._stat_n = 128

            self._emit()
            self.k.realize()
        return nc

    def ps(self, bank, n=512, off=0):
        return self.PS[:, bank * 512 + off: bank * 512 + off + n]

    def psb(self, bank, nbf=1024, off=0):
        return self.PS[:, bank * 512:(bank + 1) * 512].bitcast(BF16)[:, off:off + nbf]

    def ar(self, off, n, dt=BF16, parts=(0, 128)):
        v = self.AR[parts[0]:parts[1], off:off + n]
        if dt is BF16:
            return v
        return v.bitcast(F32)

    def stat(self, n=1):
        c = self._stat_n
        assert c + n <= 192
        self._stat_n += n
        return self.STAT[:, c:c + n]

    def stat_reset(self):
        self._stat_n = 128

    def bcast_row(self, bank, row_ap, n, pb=0):
        k, bt = self.k, self.bt
        hi = self.ZROWS[pb:pb + 1, 0:n]
        lo = self.ZROWS[pb:pb + 1, 1024:1024 + n]
        k.wait("vector", getattr(self, "_zrow_rd", None))
        k.wait_self("vector")
        th = k.sig("vector", lambda e: e.tensor_copy(hi, row_ap))
        k.wait_self("vector")
        tl = k.sig("vector", lambda e: e.tensor_sub(lo, row_ap, hi))
        bt.pre_mm(bank)
        k.wait("tensor", tl)
        k.emit("tensor", lambda e: e.matmul(
            self.ps(bank, n), self.ONB[pb:pb + 1, :], hi, start=True, stop=False))
        tm = k.sig("tensor", lambda e: e.matmul(
            self.ps(bank, n), self.ONB[pb:pb + 1, :], lo, start=False, stop=True))
        bt.set_w(bank, tm)
        self._zrow_rd = tm
        return tm

    def tm2fm(self, src_fn, dst_fn, nch, bank=7):
        k, bt = self.k, self.bt
        for g0 in range(0, nch, 4):
            n = min(4, nch - g0)
            bt.pre_mm(bank)
            tt = None
            for j in range(n):
                c = g0 + j
                fn = (lambda e, c=c, j=j: e.transpose(
                    self.psb(bank, 128, j * 128), src_fn(c), self.IDB[:, :]))
                if j == n - 1:
                    tt = k.sig("tensor", fn)
                else:
                    k.emit("tensor", fn)
            bt.set_w(bank, tt)
            bt.pre_read(bank, "vector")
            tv = k.sig("vector", lambda e, g0=g0, n=n: e.tensor_copy(
                dst_fn(g0, n), self.psb(bank, n * 128, 0)))
            bt.add_r(bank, tv)
        k.wait("tensor", tv)
        return tv

    def ln_post(self, src_ap, t, dst_ap=None):
        k = self.k
        ssq = self.stat(); sx_ = self.stat(); mu = self.stat(); nm2 = self.stat()
        va = self.stat(); ex2 = self.stat(); rq = self.stat(); nmr = self.stat()
        rva = self.stat()
        k.wait("scalar", k.tok("vector"))
        k.wait("vector", k.tok("scalar"))
        t1 = k.sig("scalar", lambda e: e.activation(
            self.SCRC[:, 0:D], src_ap, AF.Square))
        self._ln_src_rd = t1
        k.wait_self("vector")
        t2 = k.sig("vector", lambda e: e.reduce_sum(sx_, src_ap, axis=AX.X))
        k.wait("vector", t1)
        k.sig("vector", lambda e: e.reduce_sum(ssq, self.SCRC[:, 0:D], axis=AX.X))
        k.wait_self("vector")
        k.sig("vector", lambda e: e.tensor_scalar(mu, sx_, 1.0 / D, None, ALU.mult))
        k.sig("vector", lambda e: e.tensor_scalar(ex2, ssq, 1.0 / D, 1e-5, ALU.mult, ALU.add))
        k.wait_self("vector")
        k.sig("vector", lambda e: e.scalar_tensor_tensor(nm2, mu, -1.0, mu, ALU.mult, ALU.mult))
        k.wait_self("vector")
        k.sig("vector", lambda e: e.scalar_tensor_tensor(va, nm2, 1.0, ex2, ALU.mult, ALU.add))
        k.wait_self("vector")
        t3 = k.sig("vector", lambda e: e.reciprocal(rva, va))
        k.wait("scalar", t3)
        t4 = k.sig("scalar", lambda e: e.activation(rq, rva, AF.Sqrt))
        k.wait("vector", t4)
        k.sig("vector", lambda e: e.scalar_tensor_tensor(nmr, mu, -1.0, rq, ALU.mult, ALU.mult))
        k.wait_self("vector")
        if dst_ap is None:
            dst_ap = self.X[:, t * D:(t + 1) * D]
        return k.sig("vector", lambda e, dst_ap=dst_ap: e.tensor_scalar(
            dst_ap, src_ap, rq, nmr, ALU.mult, ALU.add))

    def _emit(self):
        k = self.k
        d = self.dram
        for t in range(NT):
            k.dma("sync", self.X[:, t * D:(t + 1) * D],
                  d["x"][t * 128:(t + 1) * 128, :])
        k.dma("sync", self.IDB[:, :], d["identb"][:, :])
        k.dma("sync", self.IDF[:, :], d["identf"][:, :])
        k.dma("sync", self.ONB[:, :], d["onesb"][:, :])
        k.dma("sync", self.ONF[:, :], d["onesf"][:, :])
        k.dma("sync", self.MKB[:, 0:2048], d["l0mask"][:, :])
        k.dma("sync", self.MKB[:, 2048:2176], d["l1maskd"][:, :])
        k.dma("sync", self.MKB[:, 2176:2304], d["l1maskp"][:, :])
        k.dma("sync", self.MKB[:, 2304:2432], d["l2triu"][:, :])
        k.barrier()

        for b in self.blocks:
            getattr(self, "emit_" + b)()
            k.barrier()
            self.stat_reset()

        for t in range(NT):
            k.dma("scalar", self.out[t * 128:(t + 1) * 128, :],
                  self.X[:, t * D:(t + 1) * D])
        k.wait_self("scalar")

    def emit_l0a(self):
        k, bt, d = self.k, self.bt, self.dram
        KFM = 0
        VT = 16384
        QFM = 33024
        PT = 37120
        AVF = 38144
        WS = 42240
        XFMG = 50432
        XB = 38144
        TMP = 50432

        KM = self.ROWS[0:1, 0:16]
        ZRa = self.ROWS[32:33, 0:512]
        ZRb = self.ROWS[64:65, 0:512]
        self._wtoks = []
        self._bq_dma = []

        tb1 = k.dma("gpsimd", self.ROWSB[64:65, 0:3072], d["l0_bqkv"][:, :])
        tb2 = k.dma("gpsimd", self.ROWSB[64:65, 3072:4096], d["l0_bout"][:, :])
        tva = k.sig("vector", lambda e: e.memset(
            self.ar(VT, 16 * 1040).rearrange("p (u d) -> p u d", d=65)[:, :, 64:65], 1.0))
        tkm = k.sig("vector", lambda e: e.memset(KM, 0.0))
        k.sig("vector", lambda e: e.memset(self.ROWS[0:32, 20:21], 0.0))

        import os as _os
        for g in range(int(_os.environ.get('L0A_NG', '4'))):
            tiles = [4 * g + j for j in range(4)]
            k.wait("vector", k.tok("scalar"))
            k.wait("vector", k.tok("tensor"))
            for j, t in enumerate(tiles):
                k.wait("vector", getattr(self, "_xb_rd", None))
                tc = k.sig("vector", lambda e, t=t: e.tensor_copy(
                    self.ar(XB, 1024), self.X[:, t * D:(t + 1) * D]))
                k.wait("tensor", tc)
                self.tm2fm(lambda c: self.ar(XB + c * 128, 128),
                           lambda g0, n, j=j: self.ar(XFMG + j * 1024 + g0 * 128, n * 128),
                           8)
                self._xb_rd = k.tok("tensor")
                self._stg_rd = k.tok("tensor")
            for s in range(6):
                wsl = WS + (s % 2) * 4096
                if len(self._wtoks) >= 2:
                    k.wait("gpsimd", self._wtoks[-2])
                tw = k.dma("gpsimd",
                           self.ar(wsl, 4096).rearrange("p (c n) -> p c n", n=512),
                           d["l0_wqkv"].rearrange("(c p) m -> p c m", p=128)[:, :, s * 512:(s + 1) * 512])
                for j, t in enumerate(tiles):
                    bk = (s * 4 + j) % 2
                    bt.pre_mm(bk)
                    k.wait("tensor", tw)
                    for c in range(8):
                        k.emit("tensor", lambda e, c=c, j=j, wsl=wsl, bk=bk: e.matmul(
                            self.ps(bk), self.ar(XFMG + j * 1024 + c * 128, 128),
                            self.ar(wsl + c * 512, 512),
                            start=(c == 0), stop=False))
                    k.wait("tensor", tb1)
                    tq = k.sig("tensor", lambda e, s=s, bk=bk: e.matmul(
                        self.ps(bk), self.ONB[64:65, :],
                        self.ROWSB[64:65, s * 512:(s + 1) * 512],
                        start=False, stop=True))
                    bt.set_w(bk, tq)
                    qn = self.STAT[:, j * 16:j * 16 + 16]
                    kn = self.STAT[:, 64 + j * 16:64 + j * 16 + 16]
                    self._l0a_evac(s, j, t, bk, KFM, VT, QFM, qn, kn)
                self._wtoks.append(tq)
            for j, t in enumerate(tiles):
                qn = self.STAT[:, j * 16:j * 16 + 16]
                kn = self.STAT[:, 64 + j * 16:64 + j * 16 + 16]
                m16 = self.STAT[:, 160:176]
                self._l0a_bq(j, t, qn, kn, m16, KM)
            k.wait("tensor", k.tok("vector"))
            k.wait("tensor", k.tok("scalar"))
            nkc = 4 * g + 4
            avb = [2, 3]
            for h in range(16):
                c, hp = h // 2, (h % 2) * 64
                ab = avb[h % 2]
                bt.pre_mm(ab)
                tav = None
                for kc in range(nkc):
                    sbk = 4 + (kc % 2)
                    bt.pre_mm(sbk)
                    k.emit("tensor", lambda e, c=c, hp=hp, kc=kc, g=g, sbk=sbk: e.matmul(
                        self.ps(sbk), self.ar(KFM + c * 2048 + kc * 128, 128)[hp:hp + 64, :],
                        self.ar(QFM + c * 512, 512)[hp:hp + 64, :],
                        start=True, stop=False))
                    br = 0 if h < 8 else 32
                    hh = h % 8
                    tsc = k.sig("tensor", lambda e, br=br, hh=hh, g=g, sbk=sbk: e.matmul(
                        self.ps(sbk), self.ONB[br:br + 1, :],
                        self.ROWSB[br:br + 1, hh * 512:(hh + 1) * 512],
                        start=False, stop=True))
                    bt.set_w(sbk, tsc)
                    bt.pre_read(sbk, "scalar")
                    pslot = PT + (kc % 2) * 512
                    texp = k.sig("scalar", lambda e, pslot=pslot, sbk=sbk: e.activation(
                        self.ar(pslot, 512), self.ps(sbk), AF.Exp))
                    bt.add_r(sbk, texp)
                    tmask = texp
                    if kc >= 4 * g:
                        jj = kc - 4 * g
                        k.wait("vector", texp)
                        tmask = k.sig("vector", lambda e, pslot=pslot, jj=jj: e.tensor_tensor(
                            self.ar(pslot, 512), self.ar(pslot, 512),
                            self.MKB[:, jj * 512:(jj + 1) * 512], ALU.mult))
                    k.wait("tensor", tmask)
                    k.wait("tensor", tva)
                    fn = (lambda e, kc=kc, h=h, pslot=pslot, ab=ab, nkc=nkc: e.matmul(
                        self.PS[0:65, ab * 512:ab * 512 + 512],
                        self.ar(VT + kc * 1040 + h * 65, 65),
                        self.ar(pslot, 512),
                        start=(kc == 0), stop=(kc == nkc - 1)))
                    if kc == nkc - 1:
                        tav = k.sig("tensor", fn)
                    else:
                        k.emit("tensor", fn)
                bt.set_w(ab, tav)
                bt.pre_read(ab, "vector")
                tz = k.sig("vector", lambda e, ab=ab: e.tensor_copy(
                    ZRa, self.PS[64:65, ab * 512:ab * 512 + 512]))
                bt.add_r(ab, tz)
                k.wait_self("vector")
                tz2 = k.sig("vector", lambda e: e.reciprocal(ZRb, ZRa))
                zb = 6
                k.wait("vector", tz2)
                self.bcast_row(zb, ZRb, 512, pb=64)
                bt.pre_read(zb, "vector")
                tzc = k.sig("vector", lambda e, zb=zb: e.tensor_copy(
                    self.SCRD[:, 0:512], self.ps(zb)))
                bt.add_r(zb, tzc)
                k.wait_self("vector")
                if h % 2 == 0:
                    tev = k.sig("vector", lambda e, ab=ab, c=c: e.tensor_tensor(
                        self.ar(AVF + c * 512, 512)[0:64, :],
                        self.PS[0:64, ab * 512:ab * 512 + 512],
                        self.SCRD[0:64, 0:512], ALU.mult))
                    bt.add_r(ab, tev)
                else:
                    tev = k.sig("vector", lambda e, ab=ab: e.tensor_tensor(
                        self.ar(TMP, 512)[0:64, :],
                        self.PS[0:64, ab * 512:ab * 512 + 512],
                        self.SCRD[0:64, 0:512], ALU.mult))
                    bt.add_r(ab, tev)
                    k.wait("scalar", tev)
                    tev = k.sig("scalar", lambda e, c=c: e.activation(
                        self.ar(AVF + c * 512, 512, parts=(64, 128)),
                        self.ar(TMP, 512)[0:64, :], AF.Copy))
            self._bias_rd = tsc
            k.wait("tensor", k.tok("vector"))
            k.wait("tensor", k.tok("scalar"))
            k.wait("gpsimd", self._wtoks[-1])
            tw = k.dma("gpsimd",
                       self.ar(WS, 8192).rearrange("p (c n) -> p c n", n=1024),
                       d["l0_wout"].rearrange("(c p) m -> p c m", p=128))
            k.wait("tensor", tb2)
            for j, t in enumerate(tiles):
                for s2 in range(2):
                    bk = s2
                    bt.pre_mm(bk)
                    k.wait("tensor", tw)
                    for c in range(8):
                        k.emit("tensor", lambda e, c=c, j=j, s2=s2, bk=bk: e.matmul(
                            self.ps(bk), self.ar(AVF + c * 512 + j * 128, 128),
                            self.ar(WS + c * 1024 + s2 * 512, 512),
                            start=(c == 0), stop=False))
                    to = k.sig("tensor", lambda e, s2=s2, bk=bk: e.matmul(
                        self.ps(bk), self.ONB[64:65, :],
                        self.ROWSB[64:65, 3072 + s2 * 512:3072 + (s2 + 1) * 512],
                        start=False, stop=True))
                    bt.set_w(bk, to)
                k.wait("vector", getattr(self, "_ln_src_rd", None))
                for s2 in range(2):
                    bt.pre_read(s2, "vector")
                    tr = k.sig("vector", lambda e, s2=s2, t=t: e.tensor_add(
                        self.SCRA[:, s2 * 512:(s2 + 1) * 512], self.ps(s2),
                        self.X[:, t * D + s2 * 512:t * D + (s2 + 1) * 512]))
                    bt.add_r(s2, tr)
                k.wait("scalar", tr); k.wait("vector", tr)
                tln = self.ln_post(self.SCRA[:, 0:D], t)
                k.wait("tensor", tln); k.wait("vector", tln)
                self.stat_reset()
            self._wtoks.append(to)
            self._wtoks.append(to)

    def _l0a_evac(self, s, j, t, bk, KFM, VT, QFM, qn, kn):
        k, bt = self.k, self.bt
        XB = 54528
        if s >= 4:
            bt.pre_read(bk, "vector")
            tv = k.sig("vector", lambda e, t=t, s=s, bk=bk: e.tensor_copy(
                self.ar(VT + t * 1040 + (s - 4) * 8 * 65, 520).rearrange(
                    "p (u d) -> p u d", d=65)[:, :, 0:64],
                self.ps(bk).rearrange("p (u d) -> p u d", d=64)))
            bt.add_r(bk, tv)
            return
        scale = 0.125 if s < 2 else 1.0
        VTb = 38144
        bt.pre_read(bk, "scalar")
        k.wait("scalar", getattr(self, "_stg_rd", None))
        tc = k.sig("scalar", lambda e, bk=bk, scale=scale: e.activation(
            self.ar(VTb, 512), self.ps(bk), AF.Copy, scale=scale))
        bt.add_r(bk, tc)
        k.wait("scalar", getattr(self, "_scrc_rd", None))
        t2 = k.sig("scalar", lambda e: e.activation(
            self.SCRC[:, 0:512], self.ar(VTb, 512), AF.Square))
        dst = qn if s < 2 else kn
        off = (s % 2) * 8
        k.wait("vector", t2)
        t3 = k.sig("vector", lambda e, dst=dst, off=off: e.reduce_sum(
            dst[:, off:off + 8].rearrange("p (h one) -> p h one", one=1),
            self.SCRC[:, 0:512].rearrange("p (h e) -> p h e", e=64), axis=AX.X))
        self._scrc_rd = t3
        if not hasattr(self, "_stat_tok"):
            self._stat_tok = {}
        self._stat_tok[j] = t3
        tt = None
        for c2 in range(4):
            if s < 2:
                cglob = s * 4 + c2
                dst_ap = self.ar(QFM + cglob * 512 + j * 128, 128)
            else:
                cglob = (s - 2) * 4 + c2
                dst_ap = self.ar(KFM + cglob * 2048 + t * 128, 128)
            bt.pre_mm(7)
            k.wait("tensor", tc)
            tt = k.sig("tensor", lambda e, c2=c2: e.transpose(
                self.psb(7, 128, 0), self.ar(VTb + c2 * 128, 128), self.IDB[:, :]))
            bt.set_w(7, tt)
            bt.pre_read(7, "vector")
            tv = k.sig("vector", lambda e, dst_ap=dst_ap: e.tensor_copy(
                dst_ap, self.psb(7, 128, 0)))
            bt.add_r(7, tv)
        self._stg_rd = tt

    def _l0a_bq(self, j, t, qn, kn, m16, KM):
        k, bt = self.k, self.bt
        qns = self.SCRD[:, 16:32]
        kns = self.SCRD[:, 32:48]
        bq = self.SCRD[:, 48:64]
        ktmp = self.SCRD[:, 0:16]
        m16b = self.STAT[:, 176:192]
        for tok in getattr(self, "_bq_war", ()):
            k.wait("scalar", tok)
        for tok in getattr(self, "_bq_war_v", ()):
            k.wait("vector", tok)
        k.wait("scalar", self._stat_tok.get(j))
        t1 = k.sig("scalar", lambda e: e.activation(qns, qn, AF.Sqrt))
        t2 = k.sig("scalar", lambda e: e.activation(kns, kn, AF.Sqrt))
        k.wait("vector", t2)
        k.sig("vector", lambda e: e.tensor_copy(ktmp[0:64, :], kns[64:128, :]))
        k.wait_self("vector")
        k.sig("vector", lambda e: e.tensor_tensor(m16[0:64, :], kns[0:64, :], ktmp[0:64, :], ALU.max))
        k.wait_self("vector")
        k.sig("vector", lambda e: e.tensor_copy(ktmp[0:32, :], m16[32:64, :]))
        k.wait_self("vector")
        tf = k.sig("vector", lambda e: e.tensor_tensor(m16b[0:32, :], m16[0:32, :], ktmp[0:32, :], ALU.max))
        bt.pre_mm(6)
        k.wait("tensor", tf)
        tt = k.sig("tensor", lambda e: e.transpose(
            self.PS[0:16, 6 * 512:6 * 512 + 32], m16b[0:32, :], self.IDF[0:32, 0:32]))
        bt.set_w(6, tt)
        bt.pre_read(6, "vector")
        knmax = self.ROWS[0:16, 20:21]
        tr = k.sig("vector", lambda e: e.reduce_max(
            knmax, self.PS[0:16, 6 * 512:6 * 512 + 32], axis=AX.X))
        bt.add_r(6, tr)
        bt.pre_mm(6)
        k.wait("tensor", tr)
        tt2 = k.sig("tensor", lambda e: e.transpose(
            self.PS[0:1, 6 * 512:6 * 512 + 32], self.ROWS[0:32, 20:21],
            self.IDF[0:32, 0:32]))
        bt.set_w(6, tt2)
        bt.pre_read(6, "vector")
        KMn = self.ROWS[0:1, 24:40]
        tr2 = k.sig("vector", lambda e, KMn=KMn: e.tensor_tensor(
            KMn, KM, self.PS[0:1, 6 * 512:6 * 512 + 16], ALU.max))
        bt.add_r(6, tr2)
        k.wait_self("vector")
        tr3 = k.sig("vector", lambda e, KMn=KMn: e.tensor_copy(KM, KMn))
        k.wait("vector", tr3)
        self.bcast_row(6, KM, 16, pb=0)
        bt.pre_read(6, "vector")
        k.wait("vector", t1)
        tb = k.sig("vector", lambda e, bq=bq: e.scalar_tensor_tensor(
            bq, qns, -1.0, self.PS[:, 6 * 512:6 * 512 + 16], ALU.mult, ALU.mult))
        bt.add_r(6, tb)
        import os as _os
        if _os.environ.get("L0A_DBG") and not getattr(self, "_dbg_done", False):
            self._dbg_done = True
            td1 = k.sig("vector", lambda e: e.tensor_copy(
                self.SCRB[:, 0:16], self.PS[:, 6 * 512:6 * 512 + 16]))
            bt.add_r(6, td1)
            td2 = k.sig("vector", lambda e, bq=bq: e.tensor_copy(self.SCRB[:, 16:32], bq))
            td3 = k.sig("vector", lambda e: e.tensor_copy(self.SCRB[:, 32:48], qn))
            td4 = k.sig("vector", lambda e, qns=qns: e.tensor_copy(self.SCRB[:, 48:64], qns))
        bt.pre_mm(6)
        k.wait("tensor", tb)
        tt4 = k.sig("tensor", lambda e, bq=bq: e.transpose(
            self.PS[0:16, 6 * 512:6 * 512 + 128], bq, self.IDF[:, :]))
        bt.set_w(6, tt4)
        bt.pre_read(6, "vector")
        stg = self.STG[0:16, :]
        for tok in self._bq_dma:
            k.wait("vector", tok)
        tv = k.sig("vector", lambda e, stg=stg: e.tensor_copy(
            stg, self.PS[0:16, 6 * 512:6 * 512 + 128]))
        bt.add_r(6, tv)
        k.wait("scalar", tv)
        k.wait("scalar", getattr(self, "_bias_rd", None))
        jj = j
        t5 = k.dma("scalar", self.ROWSB[0:1, 0:4096].rearrange(
            "p (h q) -> p h q", q=512)[:, :, jj * 128:jj * 128 + 128],
            self.STG[0:8, :].rearrange("h (one q) -> h one q", one=1))
        t6 = k.dma("scalar", self.ROWSB[32:33, 0:4096].rearrange(
            "p (h q) -> p h q", q=512)[:, :, jj * 128:jj * 128 + 128],
            self.STG[8:16, :].rearrange("h (one q) -> h one q", one=1))
        self._bq_dma = [t5, t6]
        self._bq_war = (tb, tf, tt4)
        self._bq_war_v = (tt, tt2, tt4)
        k.wait("tensor", t5)
        k.wait("tensor", t6)


    def emit_l0f(self):
        k, bt, d = self.k, self.bt, self.dram
        HB = [0, 16448]
        XQF = 32896
        W1S = 36992
        W2S = 39040
        XQB = 40064
        F32A = 0
        CW = self.CWT[:, :]
        CB = self.CBT[:, :]
        k.dma("sync", CW, d["l0_convw"][:, :])
        k.dma("sync", CB, d["l0_convb"][:, :])

        ws1rep, iws1rep = self._l0f_quant_prep(d["l0_w1t"], self.d_w1q, 8, 4096,
                                               F32A, XQB, 0)
        ws2rep, iws2rep = self._l0f_quant_prep(d["l0_w2t"], self.d_w2q, 32, 1024,
                                               F32A, XQB, 8)
        k.barrier()

        for tg in range(4):
            cur = HB[tg % 2]
            prv = HB[(tg + 1) % 2]
            for i in range(4):
                t = tg * 4 + i
                m = self.STAT[:, 16 + i * 4:17 + i * 4]
                mc = self.STAT[:, 17 + i * 4:18 + i * 4]
                r = self.STAT[:, 18 + i * 4:19 + i * 4]
                m127 = self.STAT[:, 19 + i * 4:20 + i * 4]
                k.wait_self("vector")
                k.sig("vector", lambda e, t=t, m=m: e.reduce_max(
                    m, self.X[:, t * D:(t + 1) * D], axis=AX.X,
                    apply_absolute_value=True))
                k.wait_self("vector")
                k.sig("vector", lambda e, m=m, mc=mc: e.tensor_scalar(
                    mc, m, 1e-5, None, ALU.max))
                k.wait_self("vector")
                k.sig("vector", lambda e, r=r, mc=mc: e.reciprocal(r, mc))
                k.sig("vector", lambda e, mc=mc, m127=m127: e.tensor_scalar(
                    m127, mc, 1.0 / 127.0, None, ALU.mult))
                k.wait_self("vector")
                k.wait_self("vector")
                k.sig("vector", lambda e, t=t, r=r: e.tensor_scalar(
                    self.SCRC[:, 0:D], self.X[:, t * D:(t + 1) * D], r, 127.0,
                    ALU.mult, ALU.mult))
                k.wait_self("vector")
                k.sig("vector", lambda e: e.tensor_scalar(
                    self.SCRB[:, 0:D], self.SCRC[:, 0:D], RNE, -RNE, ALU.add, ALU.add))
                k.wait_self("vector")
                tq = k.sig("vector", lambda e, m127=m127: e.tensor_scalar(
                    self.ar(XQB, 1024), self.SCRB[:, 0:D], m127, None, ALU.mult))
                k.wait("tensor", tq)
                for c in range(8):
                    bt.pre_mm(7)
                    tt = k.sig("tensor", lambda e, c=c: e.transpose(
                        self.psb(7, 128, 0), self.ar(XQB + c * 128, 128), self.IDB[:, :]))
                    bt.set_w(7, tt)
                    bt.pre_read(7, "vector")
                    tv = k.sig("vector", lambda e, c=c, i=i: e.tensor_copy(
                        self.ar(XQF + c * 512 + i * 128, 128), self.psb(7, 128, 0)))
                    bt.add_r(7, tv)
            k.wait("tensor", k.tok("vector"))
            wtoks = [None, None]
            k.wait("scalar", getattr(self, "_l0f_hb_rd", None))
            for hc in range(32):
                slot = W1S + (hc % 2) * 1024
                if wtoks[hc % 2] is not None:
                    k.wait("sync", wtoks[hc % 2])
                tw = k.dma("sync", self.ar(slot, 1024).rearrange("p (c n) -> p c n", n=128),
                           self.d_w1q.rearrange("(c p) h -> p c h", p=128)[:, :, hc * 128:(hc + 1) * 128])
                bk = hc % 2
                bt.pre_mm(bk)
                k.wait("tensor", tw)
                tm = None
                for c in range(8):
                    fn = (lambda e, c=c, slot=slot, bk=bk: e.matmul(
                        self.ps(bk), self.ar(slot + c * 128, 128),
                        self.ar(XQF + c * 512, 512),
                        start=(c == 0), stop=(c == 7)))
                    tm = k.sig("tensor", fn) if c == 7 else k.emit("tensor", fn)
                bt.set_w(bk, tm)
                wtoks[hc % 2] = tm
                bt.pre_read(bk, "scalar")
                te = k.sig("scalar", lambda e, hc=hc, cur=cur, bk=bk, ws1rep=ws1rep: e.activation(
                    self.ar(cur + hc * 514 + 1, 512), self.ps(bk), AF.Copy, scale=ws1rep))
                bt.add_r(bk, te)
            h_tok = k.tok("scalar")
            k.wait("vector", h_tok)
            if tg == 0:
                k.sig("vector", lambda e, cur=cur: e.memset(
                    self.ar(cur, 32 * 514).rearrange("p (c w) -> p c w", w=514)[:, :, 0:1], 0.0))
            else:
                k.sig("vector", lambda e, cur=cur, prv=prv: e.tensor_copy(
                    self.ar(cur, 32 * 514).rearrange("p (c w) -> p c w", w=514)[:, :, 0:1],
                    self.ar(prv, 32 * 514).rearrange("p (c w) -> p c w", w=514)[:, :, 512:513]))
                k.sig("vector", lambda e, cur=cur, prv=prv: e.tensor_copy(
                    self.ar(prv, 32 * 514).rearrange("p (c w) -> p c w", w=514)[:, :, 513:514],
                    self.ar(cur, 32 * 514).rearrange("p (c w) -> p c w", w=514)[:, :, 1:2]))
                self._l0f_tail(tg - 1, prv, iws1rep and None, ws2rep, iws2rep, XQF, W2S)
            if tg == 3:
                k.wait_self("vector")
                k.sig("vector", lambda e, cur=cur: e.memset(
                    self.ar(cur, 32 * 514).rearrange("p (c w) -> p c w", w=514)[:, :, 513:514], 0.0))
                self._l0f_tail(3, cur, None, ws2rep, iws2rep, XQF, W2S)

    def _l0f_quant_prep(self, wdram, qdram, nch, ncols, F32A, XQB, statbase):
        k, bt, d = self.k, self.bt, self.dram
        nb = ncols * 2
        slots = [F32A, F32A + 8192]
        dtoks = [None, None]
        for ch in range(nch):
            sl = slots[ch % 2]
            if dtoks[ch % 2] is not None:
                k.wait("sync", dtoks[ch % 2])
            tw = k.dma("sync", self.ar(sl, nb, F32), wdram[ch * 128:(ch + 1) * 128, :])
            k.wait("vector", tw)
            k.wait_self("vector")
            tv = k.sig("vector", lambda e, sl=sl, ch=ch: e.reduce_sum(
                self.STAT[:, 32 + ch % 32:33 + ch % 32] if False else self.STAT[:, 32 + ch:33 + ch] if nch <= 8 else self.STAT[:, 32 + (ch % 32):33 + (ch % 32)],
                self.ar(sl, nb, F32), axis=AX.X, apply_absolute_value=True))
            dtoks[ch % 2] = tv
            if nch > 8 and ch % 32 == 31:
                k.wait_self("vector")
                tv2 = k.sig("vector", lambda e, ch=ch: e.reduce_sum(
                    self.STAT[:, 30 + ch // 32:31 + ch // 32], self.STAT[:, 32:64], axis=AX.X))
        k.wait_self("vector")
        if nch <= 8:
            tcs = k.sig("vector", lambda e: e.reduce_sum(
                self.STAT[:, 28:29], self.STAT[:, 32:32 + nch], axis=AX.X))
        else:
            tcs = k.sig("vector", lambda e: e.reduce_sum(
                self.STAT[:, 28:29], self.STAT[:, 30:31], axis=AX.X))
        bt.pre_mm(5)
        k.wait("tensor", tcs)
        tms = k.sig("tensor", lambda e: e.matmul(
            self.PS[0:1, 5 * 512:5 * 512 + 1], self.STAT[:, 28:29], self.ONF[:, 0:1],
            start=True, stop=True))
        bt.set_w(5, tms)
        bt.pre_read(5, "vector")
        wsr = self.ROWS[0:1, 100 + statbase:101 + statbase]
        iwsr = self.ROWS[0:1, 101 + statbase:102 + statbase]
        tv = k.sig("vector", lambda e, wsr=wsr: e.tensor_scalar(
            wsr, self.PS[0:1, 5 * 512:5 * 512 + 1], 1.0 / (nch * 128.0 * ncols), None, ALU.mult))
        bt.add_r(5, tv)
        k.wait_self("vector")
        k.sig("vector", lambda e, wsr=wsr: e.tensor_scalar(wsr, wsr, 1e-5, None, ALU.max))
        k.wait_self("vector")
        k.sig("vector", lambda e, wsr=wsr, iwsr=iwsr: e.reciprocal(iwsr, wsr))
        k.wait_self("vector")
        self.bcast_row(5, wsr, 1, pb=0)
        bt.pre_read(5, "vector")
        wsrep = self.STAT[:, 4 + statbase // 4:5 + statbase // 4]
        iwsrep = self.STAT[:, 5 + statbase // 4:6 + statbase // 4]
        tv1 = k.sig("vector", lambda e, wsrep=wsrep: e.tensor_copy(
            wsrep, self.PS[:, 5 * 512:5 * 512 + 1]))
        bt.add_r(5, tv1)
        self.bcast_row(5, iwsr, 1, pb=0)
        bt.pre_read(5, "vector")
        tv2 = k.sig("vector", lambda e, iwsrep=iwsrep: e.tensor_copy(
            iwsrep, self.PS[:, 5 * 512:5 * 512 + 1]))
        bt.add_r(5, tv2)
        dtoks = [None, None]
        qtok = None
        TSL = 16384
        QB = 40960
        for ch in range(nch):
            sl = slots[ch % 2]
            if dtoks[ch % 2] is not None:
                k.wait("sync", dtoks[ch % 2])
            tw = k.dma("sync", self.ar(sl, nb, F32), wdram[ch * 128:(ch + 1) * 128, :])
            k.wait("vector", tw)
            k.wait("vector", qtok)
            k.wait_self("vector")
            tq0 = k.sig("vector", lambda e, sl=sl, iwsrep=iwsrep: e.tensor_scalar(
                self.ar(TSL, nb, F32), self.ar(sl, nb, F32), iwsrep, RNE, ALU.mult, ALU.add))
            dtoks[ch % 2] = tq0
            k.wait_self("vector")
            k.sig("vector", lambda e, sl=sl: e.tensor_scalar(
                self.ar(TSL + 8192, nb, F32) if nb <= 8192 else None,
                self.ar(TSL, nb, F32), -RNE, 1.0, ALU.add, ALU.min))
            k.wait_self("vector")
            tq = k.sig("vector", lambda e: e.tensor_scalar(
                self.ar(QB, ncols), self.ar(TSL + 8192, nb, F32), -1.0, None, ALU.max))
            k.wait("scalar", tq)
            qtok = k.dma("scalar", qdram[ch * 128:(ch + 1) * 128, :], self.ar(QB, ncols))
        k.wait("vector", qtok)
        k.barrier()
        return wsrep, iwsrep

    def _l0f_tail(self, tg, buf, _unused, ws2rep, iws2rep, XQF, W2S):
        k, bt, d = self.k, self.bt, self.dram
        import os as _os
        _stop = _os.environ.get("L0F_STOP", "")
        if _stop == "h":
            return
        CW = self.CWT[:, :]
        CB = self.CBT[:, :]
        hb = lambda hc, lo, n: self.ar(buf + hc * 514 + lo, n)
        for hc in range(32):
            k.wait_self("vector")
            k.wait("vector", getattr(self, "_convt_rd", None))
            k.sig("vector", lambda e, hc=hc: e.tensor_scalar(
                self.SCRD[:, 0:512], hb(hc, 0, 512), CW[:, hc:hc + 1], None, ALU.mult))
            k.wait_self("vector")
            k.sig("vector", lambda e, hc=hc: e.scalar_tensor_tensor(
                self.SCRC[:, 0:512], hb(hc, 1, 512), CW[:, 32 + hc:33 + hc],
                self.SCRD[:, 0:512], ALU.mult, ALU.add))
            k.wait_self("vector")
            tcv = k.sig("vector", lambda e, hc=hc: e.scalar_tensor_tensor(
                self.SCRD[:, 0:512], hb(hc, 2, 512), CW[:, 64 + hc:65 + hc],
                self.SCRC[:, 0:512], ALU.mult, ALU.add))
            k.wait("scalar", tcv)
            if _stop == "conv":
                ts_ = k.sig("scalar", lambda e, hc=hc: e.activation(
                    hb(hc, 1, 512), self.SCRD[:, 0:512], AF.Copy))
            else:
                ts_ = k.sig("scalar", lambda e, hc=hc: e.activation(
                    hb(hc, 1, 512), self.SCRD[:, 0:512], AF.Silu, bias=CB[:, hc:hc + 1]))
            self._convt_rd = ts_
        h2tok = k.tok("scalar")
        if _stop in ("h2", "conv"):
            return
        k.wait("vector", h2tok)
        k.wait_self("vector")
        k.sig("vector", lambda e: e.reduce_max(
            self.SCRC[:, 0:512].rearrange("p (t one) -> p t one", one=1),
            self.ar(buf, 32 * 514).rearrange("p (c w) -> p w c", w=514)[:, 1:513, :],
            axis=AX.X, apply_absolute_value=True))
        k.wait_self("vector")
        k.sig("vector", lambda e: e.tensor_copy(self.SCRB[0:64, 0:512], self.SCRC[64:128, 0:512]))
        k.wait_self("vector")
        k.sig("vector", lambda e: e.tensor_tensor(
            self.SCRC[0:64, 0:512], self.SCRC[0:64, 0:512], self.SCRB[0:64, 0:512], ALU.max))
        k.wait_self("vector")
        k.sig("vector", lambda e: e.tensor_copy(self.SCRB[0:32, 0:512], self.SCRC[32:64, 0:512]))
        k.wait_self("vector")
        tf = k.sig("vector", lambda e: e.tensor_tensor(
            self.SCRD[0:32, 0:512], self.SCRC[0:32, 0:512], self.SCRB[0:32, 0:512], ALU.max))
        M2R = self.ROWS[32:33, 0:512]
        for jj in range(4):
            bt.pre_mm(6)
            k.wait("tensor", tf)
            tt = k.sig("tensor", lambda e, jj=jj: e.transpose(
                self.PS[:, 6 * 512 + jj * 32:6 * 512 + jj * 32 + 32],
                self.SCRD[0:32, jj * 128:(jj + 1) * 128], self.IDF[0:32, 0:32]))
            bt.set_w(6, tt)
            bt.pre_read(6, "vector")
            tv = k.sig("vector", lambda e, jj=jj: e.reduce_max(
                self.STAT[:, 60 + jj:61 + jj], self.PS[:, 6 * 512 + jj * 32:6 * 512 + jj * 32 + 32],
                axis=AX.X))
            bt.add_r(6, tv)
            bt.pre_mm(6)
            k.wait("tensor", tv)
            tt2 = k.sig("tensor", lambda e, jj=jj: e.transpose(
                self.PS[0:1, 6 * 512 + 128:6 * 512 + 256], self.STAT[:, 60 + jj:61 + jj],
                self.IDF[:, :]))
            bt.set_w(6, tt2)
            bt.pre_read(6, "vector")
            tv2 = k.sig("vector", lambda e, jj=jj: e.tensor_copy(
                M2R[:, jj * 128:(jj + 1) * 128] if False else self.ROWS[32:33, jj * 128:jj * 128 + 128],
                self.PS[0:1, 6 * 512 + 128:6 * 512 + 256]))
            bt.add_r(6, tv2)
        k.wait_self("vector")
        k.sig("vector", lambda e: e.tensor_scalar(
            self.ROWS[64:65, 0:512], self.ROWS[32:33, 0:512], 1e-5, None, ALU.max))
        k.wait_self("vector")
        k.sig("vector", lambda e: e.reciprocal(self.ROWS[0:1, 0:512], self.ROWS[64:65, 0:512]))
        k.wait_self("vector")
        self.bcast_row(6, self.ROWS[0:1, 0:512], 512, pb=0)
        bt.pre_read(6, "vector")
        tv = k.sig("vector", lambda e: e.tensor_scalar(
            self.SCRC[:, 0:512], self.PS[:, 6 * 512:6 * 512 + 512], 127.0, None, ALU.mult))
        bt.add_r(6, tv)
        self.bcast_row(6, self.ROWS[64:65, 0:512], 512, pb=64)
        bt.pre_read(6, "vector")
        tv = k.sig("vector", lambda e: e.tensor_scalar(
            self.SCRB[:, 0:512], self.PS[:, 6 * 512:6 * 512 + 512], 1.0 / 127.0, None, ALU.mult))
        bt.add_r(6, tv)
        for hc in range(32):
            k.wait_self("vector")
            k.sig("vector", lambda e, hc=hc: e.tensor_mul(
                self.SCRD[:, 0:512], hb(hc, 1, 512), self.SCRC[:, 0:512]))
            k.wait_self("vector")
            k.sig("vector", lambda e: e.tensor_scalar(
                self.SCRA[:, 0:512], self.SCRD[:, 0:512], RNE, -RNE, ALU.add, ALU.add))
            k.wait_self("vector")
            k.sig("vector", lambda e, hc=hc: e.tensor_mul(
                hb(hc, 1, 512), self.SCRA[:, 0:512], self.SCRB[:, 0:512]))
        xq2tok = k.tok("vector")
        if _stop == "xq2":
            return
        k.wait("tensor", xq2tok)
        dt = [None, None]
        for hc in range(32):
            for s2 in range(2):
                sl = W2S + s2 * 512
                if dt[s2] is not None:
                    k.wait("gpsimd", dt[s2])
                tw = k.dma("gpsimd", self.ar(sl, 512),
                           self.d_w2q[hc * 128:(hc + 1) * 128, s2 * 512:(s2 + 1) * 512])
                if hc == 0:
                    for b in range(8):
                        bt.pre_mm(b)
                k.wait("tensor", tw)
                tm = None
                for tc in range(4):
                    bk = s2 * 4 + tc
                    fn = (lambda e, hc=hc, tc=tc, sl=sl, bk=bk: e.matmul(
                        self.ps(bk), hb(hc, 1 + tc * 128, 128), self.ar(sl, 512),
                        start=(hc == 0), stop=(hc == 31)))
                    tm = k.sig("tensor", fn) if (tc == 3) else k.emit("tensor", fn)
                dt[s2] = tm
                if hc == 31:
                    for b in range(s2 * 4, s2 * 4 + 4):
                        bt.set_w(b, tm)
        for tc in range(4):
            t = tg * 4 + tc
            k.wait("vector", getattr(self, "_ln_src_rd", None))
            for s2 in range(2):
                bk = s2 * 4 + tc
                bt.pre_read(bk, "vector")
                tr = k.sig("vector", lambda e, bk=bk, s2=s2, t=t, ws2rep=ws2rep: e.scalar_tensor_tensor(
                    self.SCRA[:, s2 * 512:(s2 + 1) * 512], self.ps(bk), ws2rep,
                    self.X[:, t * D + s2 * 512:t * D + (s2 + 1) * 512], ALU.mult, ALU.add))
                bt.add_r(bk, tr)
            k.wait("scalar", tr)
            tln = self.ln_post(self.SCRA[:, 0:D], t)
            k.wait("tensor", tln)
            self.stat_reset()
        self._l0f_hb_rd = k.tok("tensor")


    def emit_l1a(self):
        k, bt, d = self.k, self.bt, self.dram
        WQKV = 0; WOUT = 24576; COS = 32768; SINA = 33792
        KFMr = 34816; VTr = 36864; QFMt = 38944; XNB = 39968
        RPS = 40992; XFM = 42016; AVF = 43040; PTb = 44064; LNP = 44320
        LNW = self.ar(LNP, 2048, F32)
        LNB = self.ar(LNP + 2048, 2048, F32)
        KM = self.ROWS[0:1, 0:16]
        ZRa = self.ROWS[32:33, 0:512]
        ZRb = self.ROWS[64:65, 0:512]
        self._bq_dma = []
        self._bq_war = ()
        self._bq_war_v = ()
        self._bias_rd = None
        self._stat_tok = {}

        k.dma("gpsimd", self.ar(WQKV, 24576).rearrange("p (c n) -> p c n", n=3072),
              d["l1_wqkvt"].rearrange("(c p) m -> p c m", p=128))
        k.dma("gpsimd", self.ar(WOUT, 8192).rearrange("p (c n) -> p c n", n=1024),
              d["l1_woutt"].rearrange("(c p) m -> p c m", p=128))
        k.dma("sync", LNW, d["l1_lnw"][:, :])
        k.dma("sync", LNB, d["l1_lnb"][:, :])
        k.dma("sync", self.ar(COS, 1024), d["l1cos"][0:128, :])
        k.dma("sync", self.ar(SINA, 1024), d["l1sina"][0:128, :])
        k.sig("vector", lambda e: e.memset(
            self.ar(VTr, 2 * 1040).rearrange("p (u d) -> p u d", d=65)[:, :, 64:65], 1.0))
        k.sig("vector", lambda e: e.memset(KM, 0.0))
        k.sig("vector", lambda e: e.memset(self.ROWS[0:32, 20:21], 0.0))
        k.barrier()

        for t in range(NT):
            k.wait("vector", k.tok("tensor"))
            k.wait("vector", k.tok("scalar"))
            tl = self.ln_post(self.X[:, t * D:(t + 1) * D], t, dst_ap=self.SCRA[:, 0:D])
            k.wait_self("vector")
            k.sig("vector", lambda e: e.tensor_mul(self.SCRB[:, 0:D], self.SCRA[:, 0:D], LNW))
            k.wait_self("vector")
            k.sig("vector", lambda e: e.scalar_tensor_tensor(
                self.SCRA[:, 0:D], self.SCRB[:, 0:D], 1.0, LNB, ALU.mult, ALU.add))
            k.wait_self("vector")
            tc = k.sig("vector", lambda e: e.tensor_copy(self.ar(XNB, 1024), self.SCRA[:, 0:D]))
            k.wait("tensor", tc)
            self.tm2fm(lambda c: self.ar(XNB + c * 128, 128),
                       lambda g0, n: self.ar(XFM + g0 * 128, n * 128), 8)
            for sL in range(6):
                bk = sL % 2
                bt.pre_mm(bk)
                tm = None
                for c in range(8):
                    fn = (lambda e, c=c, sL=sL, bk=bk: e.matmul(
                        self.ps(bk), self.ar(XFM + c * 128, 128),
                        self.ar(WQKV + c * 3072 + sL * 512, 512),
                        start=(c == 0), stop=(c == 7)))
                    tm = k.sig("tensor", fn) if c == 7 else k.emit("tensor", fn)
                bt.set_w(bk, tm)
                if sL < 4:
                    dst = self.SCRB if sL < 2 else self.SCRC
                    bt.pre_read(bk, "scalar")
                    te = k.sig("scalar", lambda e, dst=dst, sL=sL, bk=bk: e.activation(
                        dst[:, (sL % 2) * 512:(sL % 2) * 512 + 512], self.ps(bk), AF.Copy))
                    bt.add_r(bk, te)
                else:
                    bt.pre_read(bk, "vector")
                    tv = k.sig("vector", lambda e, t=t, sL=sL, bk=bk: e.tensor_copy(
                        self.ar(VTr + (t % 2) * 1040 + (sL - 4) * 8 * 65, 520).rearrange(
                            "p (u dd) -> p u dd", dd=65)[:, :, 0:64],
                        self.ps(bk).rearrange("p (u dd) -> p u dd", dd=64)))
                    bt.add_r(bk, tv)
            qn = self.STAT[:, 0:16]
            kn = self.STAT[:, 64:80]
            for which in range(2):
                src = self.SCRB if which == 0 else self.SCRC
                k.wait("vector", k.tok("scalar"))
                if which == 0:
                    k.wait_self("vector")
                    k.sig("vector", lambda e, src=src: e.tensor_scalar(
                        self.SCRA[:, 0:D], src[:, 0:D], 0.125, None, ALU.mult))
                    src = self.SCRA
                sr3 = src[:, 0:D].rearrange("p (h dd) -> p h dd", dd=64)
                xn3 = self.ar(XNB, 1024).rearrange("p (h dd) -> p h dd", dd=64)
                sa3 = self.ar(SINA, 1024).rearrange("p (h dd) -> p h dd", dd=64)
                k.wait_self("vector")
                k.sig("vector", lambda e, sr3=sr3, xn3=xn3, sa3=sa3: e.tensor_mul(
                    xn3[:, :, 0:32], sr3[:, :, 32:64], sa3[:, :, 0:32]))
                k.sig("vector", lambda e, sr3=sr3, xn3=xn3, sa3=sa3: e.tensor_mul(
                    xn3[:, :, 32:64], sr3[:, :, 0:32], sa3[:, :, 32:64]))
                dstC = self.SCRB if which == 0 else self.SCRA
                k.wait_self("vector")
                k.sig("vector", lambda e, src=src, dstC=dstC: e.tensor_mul(
                    dstC[:, 0:D], src[:, 0:D], self.ar(COS, 1024)))
                k.wait_self("vector")
                trp = k.sig("vector", lambda e, dstC=dstC: e.scalar_tensor_tensor(
                    self.ar(RPS, 1024), self.ar(XNB, 1024), 1.0, dstC[:, 0:D],
                    ALU.mult, ALU.add))
                for c in range(8):
                    bt.pre_mm(7)
                    k.wait("tensor", trp)
                    tt = k.sig("tensor", lambda e, c=c: e.transpose(
                        self.psb(7, 128, 0), self.ar(RPS + c * 128, 128), self.IDB[:, :]))
                    bt.set_w(7, tt)
                    bt.pre_read(7, "vector")
                    if which == 0:
                        dap = self.ar(QFMt + c * 128, 128)
                    else:
                        dap = self.ar(KFMr + (t % 2) * 1024 + c * 128, 128)
                    tv = k.sig("vector", lambda e, dap=dap: e.tensor_copy(dap, self.psb(7, 128, 0)))
                    bt.add_r(7, tv)
            k.wait("tensor", k.tok("vector"))
            kcs0 = [t - 1, t] if t > 0 else [t]
            for h in range(16):
                c, hp = h // 2, (h % 2) * 64
                bt.pre_mm(6)
                tm = None
                for ki, kc in enumerate(kcs0):
                    fn = (lambda e, c=c, hp=hp, kc=kc, ki=ki: e.matmul(
                        self.PS[:, 6 * 512 + ki * 128:6 * 512 + ki * 128 + 128],
                        self.ar(QFMt + c * 128, 128)[hp:hp + 64, :],
                        self.ar(KFMr + (kc % 2) * 1024 + c * 128, 128)[hp:hp + 64, :],
                        start=True, stop=True))
                    tm = k.sig("tensor", fn)
                bt.set_w(6, tm)
                bt.pre_read(6, "vector")
                mcol = self.STAT[:, 32:33]
                k.wait_self("vector")
                tr = k.sig("vector", lambda e, mcol=mcol, n=len(kcs0) * 128: e.reduce_max(
                    mcol, self.PS[:, 6 * 512:6 * 512 + n], axis=AX.X))
                bt.add_r(6, tr)
                bt.pre_mm(6)
                k.wait("tensor", tr)
                tt = k.sig("tensor", lambda e, mcol=mcol: e.transpose(
                    self.PS[0:1, 6 * 512 + 256:6 * 512 + 384], mcol, self.IDF[:, :]))
                bt.set_w(6, tt)
                bt.pre_read(6, "vector")
                br = 0 if h < 8 else 32
                hh = h % 8
                k.wait("vector", getattr(self, "_bias_rd", None))
                tv = k.sig("vector", lambda e, br=br, hh=hh: e.tensor_scalar(
                    self.ROWSB[br:br + 1, hh * 512:hh * 512 + 128],
                    self.PS[0:1, 6 * 512 + 256:6 * 512 + 384], -1.0, None, ALU.mult))
                bt.add_r(6, tv)
            k.wait("tensor", k.tok("vector"))
            k.wait("tensor", k.tok("scalar"))
            kcs = [t - 1, t] if t > 0 else [t]
            avb = [2, 3]
            k.wait("tensor", k.tok("vector"))
            for h in range(16):
                c, hp = h // 2, (h % 2) * 64
                ab = avb[h % 2]
                bt.pre_mm(ab)
                tav = None
                for ki, kc in enumerate(kcs):
                    sbk = 4 + (ki % 2)
                    bt.pre_mm(sbk)
                    k.emit("tensor", lambda e, c=c, hp=hp, kc=kc, sbk=sbk: e.matmul(
                        self.PS[:, sbk * 512:sbk * 512 + 128],
                        self.ar(KFMr + (kc % 2) * 1024 + c * 128, 128)[hp:hp + 64, :],
                        self.ar(QFMt + c * 128, 128)[hp:hp + 64, :],
                        start=True, stop=False))
                    br = 0 if h < 8 else 32
                    hh = h % 8
                    tsc = k.sig("tensor", lambda e, br=br, hh=hh, sbk=sbk: e.matmul(
                        self.PS[:, sbk * 512:sbk * 512 + 128], self.ONB[br:br + 1, :],
                        self.ROWSB[br:br + 1, hh * 512:hh * 512 + 128],
                        start=False, stop=True))
                    bt.set_w(sbk, tsc)
                    bt.pre_read(sbk, "scalar")
                    pslot = PTb + (ki % 2) * 128
                    texp = k.sig("scalar", lambda e, pslot=pslot, sbk=sbk: e.activation(
                        self.ar(pslot, 128), self.PS[:, sbk * 512:sbk * 512 + 128], AF.Exp))
                    bt.add_r(sbk, texp)
                    mko = 2176 if kc != t else 2048
                    k.wait("vector", texp)
                    tmask = k.sig("vector", lambda e, pslot=pslot, mko=mko: e.tensor_tensor(
                        self.ar(pslot, 128), self.ar(pslot, 128),
                        self.MKB[:, mko:mko + 128], ALU.mult))
                    k.wait("tensor", tmask)
                    fn = (lambda e, kc=kc, h=h, pslot=pslot, ab=ab, ki=ki, last=(ki == len(kcs) - 1): e.matmul(
                        self.PS[0:65, ab * 512:ab * 512 + 128],
                        self.ar(VTr + (kc % 2) * 1040 + h * 65, 65),
                        self.ar(pslot, 128),
                        start=(ki == 0), stop=last))
                    if ki == len(kcs) - 1:
                        tav = k.sig("tensor", fn)
                    else:
                        k.emit("tensor", fn)
                bt.set_w(ab, tav)
                bt.pre_read(ab, "vector")
                tz = k.sig("vector", lambda e, ab=ab: e.tensor_copy(
                    ZRa[:, 0:128], self.PS[64:65, ab * 512:ab * 512 + 128]))
                bt.add_r(ab, tz)
                k.wait_self("vector")
                tz2 = k.sig("vector", lambda e: e.reciprocal(ZRb[:, 0:128], ZRa[:, 0:128]))
                zb = 6
                k.wait("vector", tz2)
                self.bcast_row(zb, ZRb[:, 0:128], 128, pb=64)
                bt.pre_read(zb, "vector")
                tzc = k.sig("vector", lambda e, zb=zb: e.tensor_copy(
                    self.SCRD[:, 0:128], self.PS[:, zb * 512:zb * 512 + 128]))
                bt.add_r(zb, tzc)
                k.wait_self("vector")
                if h % 2 == 0:
                    tev = k.sig("vector", lambda e, ab=ab, c=c: e.tensor_tensor(
                        self.ar(AVF + c * 128, 128)[0:64, :],
                        self.PS[0:64, ab * 512:ab * 512 + 128],
                        self.SCRD[0:64, 0:128], ALU.mult))
                    bt.add_r(ab, tev)
                else:
                    tev = k.sig("vector", lambda e, ab=ab: e.tensor_tensor(
                        self.ar(PTb, 128)[0:64, :] if False else self.ar(RPS, 128)[0:64, :],
                        self.PS[0:64, ab * 512:ab * 512 + 128],
                        self.SCRD[0:64, 0:128], ALU.mult))
                    bt.add_r(ab, tev)
                    k.wait("scalar", tev)
                    tev = k.sig("scalar", lambda e, c=c: e.activation(
                        self.ar(AVF + c * 128, 128, parts=(64, 128)),
                        self.ar(RPS, 128)[0:64, :], AF.Copy))
            self._bias_rd = tsc
            k.wait("tensor", k.tok("vector"))
            k.wait("tensor", k.tok("scalar"))
            for s2 in range(2):
                bk = s2
                bt.pre_mm(bk)
                tm = None
                for c in range(8):
                    fn = (lambda e, c=c, s2=s2, bk=bk: e.matmul(
                        self.ps(bk), self.ar(AVF + c * 128, 128),
                        self.ar(WOUT + c * 1024 + s2 * 512, 512),
                        start=(c == 0), stop=(c == 7)))
                    tm = k.sig("tensor", fn) if c == 7 else k.emit("tensor", fn)
                bt.set_w(bk, tm)
            k.wait("vector", getattr(self, "_ln_src_rd", None))
            for s2 in range(2):
                bt.pre_read(s2, "vector")
                tr = k.sig("vector", lambda e, s2=s2, t=t: e.scalar_tensor_tensor(
                    self.SCRA[:, s2 * 512:(s2 + 1) * 512],
                    self.X[:, t * D + s2 * 512:t * D + (s2 + 1) * 512], 2.0,
                    self.ps(s2), ALU.mult, ALU.add))
                bt.add_r(s2, tr)
            k.wait("scalar", tr)
            tln = self.ln_post(self.SCRA[:, 0:D], t)
            k.wait("tensor", tln)
            self.stat_reset()
            if t < NT - 1:
                k.wait("gpsimd", k.tok("vector"))
                k.dma("gpsimd", self.ar(COS, 1024), d["l1cos"][(t + 1) * 128:(t + 2) * 128, :])
                k.dma("gpsimd", self.ar(SINA, 1024), d["l1sina"][(t + 1) * 128:(t + 2) * 128, :])

    def emit_l1f(self):
        k, bt, d = self.k, self.bt, self.dram
        WQ = 0; F32A = 8192; AQB = 24576; XFM = 25600
        slots = [F32A, F32A + 2048]
        dt = [None, None]
        for ch in range(8):
            sl = slots[ch % 2]
            if dt[ch % 2] is not None:
                k.wait("sync", dt[ch % 2])
            tw = k.dma("sync", self.ar(sl, 2048, F32), d["l1_wt"][ch * 128:(ch + 1) * 128, :])
            k.wait("vector", tw)
            k.wait_self("vector")
            tv = k.sig("vector", lambda e, sl=sl, ch=ch: e.reduce_sum(
                self.STAT[:, 32 + ch:33 + ch], self.ar(sl, 2048, F32), axis=AX.X,
                apply_absolute_value=True))
            dt[ch % 2] = tv
        k.wait_self("vector")
        tcs = k.sig("vector", lambda e: e.reduce_sum(
            self.STAT[:, 28:29], self.STAT[:, 32:40], axis=AX.X))
        bt.pre_mm(5)
        k.wait("tensor", tcs)
        tms = k.sig("tensor", lambda e: e.matmul(
            self.PS[0:1, 5 * 512:5 * 512 + 1], self.STAT[:, 28:29], self.ONF[:, 0:1],
            start=True, stop=True))
        bt.set_w(5, tms)
        bt.pre_read(5, "vector")
        wsr = self.ROWS[0:1, 104:105]
        iwsr = self.ROWS[0:1, 105:106]
        ws127r = self.ROWS[0:1, 106:107]
        tv = k.sig("vector", lambda e: e.tensor_scalar(
            wsr, self.PS[0:1, 5 * 512:5 * 512 + 1], 1.0 / 1048576.0, 1e-8, ALU.mult, ALU.add))
        bt.add_r(5, tv)
        k.wait_self("vector")
        k.sig("vector", lambda e: e.reciprocal(iwsr, wsr))
        k.sig("vector", lambda e: e.tensor_scalar(ws127r, wsr, 1.0 / 127.0, None, ALU.mult))
        k.wait_self("vector")
        self.bcast_row(5, iwsr, 1, pb=0)
        bt.pre_read(5, "vector")
        iwsrep = self.STAT[:, 4:5]
        tv1 = k.sig("vector", lambda e: e.tensor_copy(iwsrep, self.PS[:, 5 * 512:5 * 512 + 1]))
        bt.add_r(5, tv1)
        self.bcast_row(5, ws127r, 1, pb=0)
        bt.pre_read(5, "vector")
        ws127rep = self.STAT[:, 5:6]
        tv2 = k.sig("vector", lambda e: e.tensor_copy(ws127rep, self.PS[:, 5 * 512:5 * 512 + 1]))
        bt.add_r(5, tv2)
        dt = [None, None]
        for ch in range(8):
            sl = slots[ch % 2]
            if dt[ch % 2] is not None:
                k.wait("sync", dt[ch % 2])
            tw = k.dma("sync", self.ar(sl, 2048, F32), d["l1_wt"][ch * 128:(ch + 1) * 128, :])
            k.wait("vector", tw)
            k.wait_self("vector")
            k.sig("vector", lambda e, sl=sl: e.tensor_scalar(
                self.ar(F32A + 4096, 2048, F32), self.ar(sl, 2048, F32), iwsrep, RNE,
                ALU.mult, ALU.add))
            k.wait_self("vector")
            k.sig("vector", lambda e: e.tensor_scalar(
                self.ar(F32A + 8192, 2048, F32), self.ar(F32A + 4096, 2048, F32),
                -RNE, 1.0, ALU.add, ALU.min))
            k.wait_self("vector")
            tq = k.sig("vector", lambda e, ch=ch: e.tensor_scalar(
                self.ar(WQ + ch * 1024, 1024), self.ar(F32A + 8192, 2048, F32),
                -1.0, None, ALU.max))
            dt[ch % 2] = tq
        k.barrier()
        for t in range(NT):
            m = self.STAT[:, 8:9]; asx = self.STAT[:, 9:10]
            r = self.STAT[:, 10:11]; scl = self.STAT[:, 11:12]
            k.wait("vector", k.tok("tensor")); k.wait("vector", k.tok("scalar"))
            k.wait_self("vector")
            k.sig("vector", lambda e, t=t, m=m: e.reduce_max(
                m, self.X[:, t * D:(t + 1) * D], axis=AX.X, apply_absolute_value=True))
            k.wait_self("vector")
            k.sig("vector", lambda e, m=m, asx=asx: e.tensor_scalar(
                asx, m, 1e-8, None, ALU.add))
            k.wait_self("vector")
            k.sig("vector", lambda e, asx=asx, r=r: e.reciprocal(r, asx))
            k.wait_self("vector")
            k.sig("vector", lambda e, asx=asx, scl=scl: e.tensor_scalar(
                scl, asx, ws127rep, None, ALU.mult))
            k.sig("vector", lambda e, t=t, r=r: e.tensor_scalar(
                self.SCRC[:, 0:D], self.X[:, t * D:(t + 1) * D], r, 127.0, ALU.mult, ALU.mult))
            k.wait_self("vector")
            k.sig("vector", lambda e: e.tensor_scalar(
                self.SCRB[:, 0:D], self.SCRC[:, 0:D], RNE, -RNE, ALU.add, ALU.add))
            k.wait_self("vector")
            tq = k.sig("vector", lambda e: e.tensor_copy(self.ar(AQB, 1024), self.SCRB[:, 0:D]))
            k.wait("tensor", tq)
            self.tm2fm(lambda c: self.ar(AQB + c * 128, 128),
                       lambda g0, n: self.ar(XFM + g0 * 128, n * 128), 8)
            for s2 in range(2):
                bk = s2
                bt.pre_mm(bk)
                tm = None
                for c in range(8):
                    fn = (lambda e, c=c, s2=s2, bk=bk: e.matmul(
                        self.ps(bk), self.ar(XFM + c * 128, 128),
                        self.ar(WQ + c * 1024 + s2 * 512, 512),
                        start=(c == 0), stop=(c == 7)))
                    tm = k.sig("tensor", fn) if c == 7 else k.emit("tensor", fn)
                bt.set_w(bk, tm)
            k.wait("vector", getattr(self, "_ln_src_rd", None))
            for s2 in range(2):
                bt.pre_read(s2, "vector")
                tr = k.sig("vector", lambda e, s2=s2, t=t, scl=scl: e.scalar_tensor_tensor(
                    self.SCRA[:, s2 * 512:(s2 + 1) * 512], self.ps(s2), scl,
                    self.X[:, t * D + s2 * 512:t * D + (s2 + 1) * 512], ALU.mult, ALU.add))
                bt.add_r(s2, tr)
            k.wait("scalar", tr)
            tln = self.ln_post(self.SCRA[:, 0:D], t)
            k.wait("tensor", tln)
            self.stat_reset()


    def emit_l2a(self):
        k, bt, d = self.k, self.bt, self.dram
        WQo, WKo, WVo, WOo = 0, 8192, 16384, 24576
        WG1, WG2 = 32768, 33280
        COS, SINA, XFM = 34304, 35328, 36352
        QFM, KFM, KTB, VTB = 37376, 38400, 39424, 40448
        EB, SSB, OB = 41472, 43520, 44032
        G1F, STG2, LNP, GTF, SST = 45056, 45184, 45312, 47360, 49408
        OBT = KTB
        GAM = self.ar(LNP, 1024)
        BET = self.ar(LNP + 1024, 1024)
        for nm, off in (("l2_wq", WQo), ("l2_wk", WKo), ("l2_wv", WVo), ("l2_wo", WOo)):
            k.dma("gpsimd", self.ar(off, 8192).rearrange("p (c n) -> p c n", n=1024),
                  d[nm].rearrange("(c p) m -> p c m", p=128))
        k.dma("gpsimd", self.ar(WG1, 512).rearrange("p (c n) -> p c n", n=64),
              d["l2_wg1"].rearrange("(c p) m -> p c m", p=128))
        k.dma("gpsimd", self.ar(WG2, 1024, parts=(0, 64)), d["l2_wg2"][:, :])
        k.dma("gpsimd", self.ar(LNP, 1024), d["l2_gamma"][:, :])
        k.dma("gpsimd", self.ar(LNP + 1024, 1024), d["l2_beta"][:, :])
        k.dma("sync", self.ar(COS, 1024), d["l2cos"][0:128, :])
        k.dma("sync", self.ar(SINA, 1024), d["l2sina"][0:128, :])
        k.sig("vector", lambda e: e.memset(self.ar(SST, 4096, F32, parts=(0, 64)), 0.0))
        k.sig("vector", lambda e: e.memset(self.ar(SSB, 512), 0.0))
        k.barrier()

        import os as _os
        _stop = _os.environ.get("L2A_STOP", "")
        for t in range(int(_os.environ.get("L2A_NT", str(NT)))):
            par = t % 2
            sstc = self.ar(SST + par * 2048, 2048, F32, parts=(0, 64))
            sstn = self.ar(SST + (1 - par) * 2048, 2048, F32, parts=(0, 64))
            k.wait("vector", k.tok("tensor")); k.wait("vector", k.tok("scalar"))
            tc = k.sig("vector", lambda e, t=t: e.tensor_copy(
                self.ar(OB, 1024), self.X[:, t * D:(t + 1) * D]))
            k.wait("tensor", tc)
            self.tm2fm(lambda c: self.ar(OB + c * 128, 128),
                       lambda g0, n: self.ar(XFM + g0 * 128, n * 128), 8)
            for which, off in (("q", WQo), ("k", WKo), ("v", WVo)):
                for s2 in range(2):
                    bk = s2
                    bt.pre_mm(bk)
                    tm = None
                    for c in range(8):
                        fn = (lambda e, c=c, s2=s2, off=off, bk=bk: e.matmul(
                            self.ps(bk), self.ar(XFM + c * 128, 128),
                            self.ar(off + c * 1024 + s2 * 512, 512),
                            start=(c == 0), stop=(c == 7)))
                        tm = k.sig("tensor", fn) if c == 7 else k.emit("tensor", fn)
                    bt.set_w(bk, tm)
                if which == "v":
                    for s2 in range(2):
                        bt.pre_read(s2, "vector")
                        k.wait("vector", getattr(self, "_l2_vtb_rd", None))
                        tv = k.sig("vector", lambda e, s2=s2: e.tensor_copy(
                            self.ar(VTB + s2 * 512, 512), self.ps(s2)))
                        bt.add_r(s2, tv)
                    continue
                for s2 in range(2):
                    bt.pre_read(s2, "scalar")
                    te = k.sig("scalar", lambda e, s2=s2: e.activation(
                        self.SCRB[:, s2 * 512:(s2 + 1) * 512], self.ps(s2), AF.Silu))
                    bt.add_r(s2, te)
                sr3 = self.SCRB[:, 0:D].rearrange("p (h dd) -> p h dd", dd=64)
                xn3 = self.SCRC[:, 0:D].rearrange("p (h dd) -> p h dd", dd=64)
                sa3 = self.ar(SINA, 1024).rearrange("p (h dd) -> p h dd", dd=64)
                k.wait("vector", te)
                k.wait_self("vector")
                k.sig("vector", lambda e, sr3=sr3, xn3=xn3, sa3=sa3: e.tensor_mul(
                    xn3[:, :, 0:32], sr3[:, :, 32:64], sa3[:, :, 0:32]))
                k.sig("vector", lambda e, sr3=sr3, xn3=xn3, sa3=sa3: e.tensor_mul(
                    xn3[:, :, 32:64], sr3[:, :, 0:32], sa3[:, :, 32:64]))
                k.wait_self("vector")
                k.sig("vector", lambda e: e.tensor_mul(
                    self.SCRA[:, 0:D], self.SCRB[:, 0:D], self.ar(COS, 1024)))
                k.wait_self("vector")
                dst_tok = KTB if which == "k" else OB
                if which == "k":
                    k.wait("vector", getattr(self, "_l2_ktb_rd", None))
                trp = k.sig("vector", lambda e, dst_tok=dst_tok: e.scalar_tensor_tensor(
                    self.ar(dst_tok, 1024), self.SCRC[:, 0:D], 1.0, self.SCRA[:, 0:D],
                    ALU.mult, ALU.add))
                dfm = QFM if which == "q" else KFM
                for c in range(8):
                    bt.pre_mm(7)
                    k.wait("tensor", trp)
                    tt = k.sig("tensor", lambda e, c=c, dst_tok=dst_tok: e.transpose(
                        self.psb(7, 128, 0), self.ar(dst_tok + c * 128, 128), self.IDB[:, :]))
                    bt.set_w(7, tt)
                    bt.pre_read(7, "vector")
                    tv = k.sig("vector", lambda e, c=c, dfm=dfm: e.tensor_copy(
                        self.ar(dfm + c * 128, 128), self.psb(7, 128, 0)))
                    bt.add_r(7, tv)
            if _stop == "proj":
                continue
            bt.pre_mm(2)
            tm = None
            for c in range(8):
                fn = (lambda e, c=c: e.matmul(
                    self.PS[:, 2 * 512:2 * 512 + 64], self.ar(XFM + c * 128, 128),
                    self.ar(WG1 + c * 64, 64), start=(c == 0), stop=(c == 7)))
                tm = k.sig("tensor", fn) if c == 7 else k.emit("tensor", fn)
            bt.set_w(2, tm)
            bt.pre_read(2, "vector")
            tg1 = k.sig("vector", lambda e: e.tensor_copy(
                self.ar(STG2, 64), self.PS[:, 2 * 512:2 * 512 + 64]))
            bt.add_r(2, tg1)
            bt.pre_mm(7)
            k.wait("tensor", tg1)
            tt = k.sig("tensor", lambda e: e.transpose(
                self.psb(7, 128, 0)[0:64, :], self.ar(STG2, 64), self.IDB[:, :]))
            bt.set_w(7, tt)
            bt.pre_read(7, "vector")
            tvg = k.sig("vector", lambda e: e.tensor_copy(
                self.ar(G1F, 128, parts=(0, 64)), self.psb(7, 128, 0)[0:64, :]))
            bt.add_r(7, tvg)
            k.wait("tensor", tvg)
            for s2 in range(2):
                bk = 2 + s2
                bt.pre_mm(bk)
                tm = k.sig("tensor", lambda e, s2=s2, bk=bk: e.matmul(
                    self.ps(bk), self.ar(G1F, 128, parts=(0, 64)),
                    self.ar(WG2 + s2 * 512, 512, parts=(0, 64)),
                    start=True, stop=True))
                bt.set_w(bk, tm)
            for s2 in range(2):
                bt.pre_read(2 + s2, "scalar")
                te = k.sig("scalar", lambda e, s2=s2: e.activation(
                    self.ar(GTF, 2048, F32)[:, s2 * 512:(s2 + 1) * 512],
                    self.ps(2 + s2), AF.Sigmoid))
                bt.add_r(2 + s2, te)
            if _stop == "gate":
                continue
            k.wait("tensor", k.tok("vector")); k.wait("tensor", k.tok("scalar"))
            for h in range(16):
                c, hp = h // 2, (h % 2) * 64
                ebk = h % 8
                if h < 8:
                    bt.pre_mm(ebk)
                fn = (lambda e, c=c, hp=hp, h=h, ebk=ebk: e.matmul(
                    self.PS[:, ebk * 512 + (h // 8) * 128:ebk * 512 + (h // 8) * 128 + 128],
                    self.ar(KFM + c * 128, 128)[hp:hp + 64, :],
                    self.ar(QFM + c * 128, 128)[hp:hp + 64, :],
                    start=(h < 8), stop=(h >= 8)))
                te = k.sig("tensor", fn)
                if h >= 8:
                    bt.set_w(ebk, te)
            if _stop == "emm":
                k.sig("vector", lambda e: e.memset(self.ar(EB, 2048), 0.0))
                continue
            for h in range(16):
                ebk = h % 8
                bt.pre_read(ebk, "vector")
                tv = k.sig("vector", lambda e, h=h, ebk=ebk: e.tensor_tensor(
                    self.ar(EB + h * 128, 128),
                    self.PS[:, ebk * 512 + (h // 8) * 128:ebk * 512 + (h // 8) * 128 + 128],
                    self.MKB[:, 2304:2432], ALU.mult))
                bt.add_r(ebk, tv)
            if _stop == "energy":
                continue
            k.wait("tensor", k.tok("vector"))
            psO_tok = None
            for h in range(16):
                c, hp = h // 2, (h % 2) * 64
                obk = 4 + h // 4
                if h % 4 == 0:
                    bt.pre_mm(obk)
                first = (h % 4 == 0)
                last = (h % 4 == 3)
                oap = self.PS[0:64, obk * 512 + (h % 4) * 128:obk * 512 + (h % 4) * 128 + 128]
                if t > 0:
                    k.emit("tensor", lambda e, h=h, oap=oap, first=first: e.matmul(
                        oap, self.ar(VTB + h * 64, 64), self.ar(EB + h * 128, 128),
                        start=first, stop=False))
                    psO_tok = k.sig("tensor", lambda e, h=h, c=c, hp=hp, oap=oap, last=last: e.matmul(
                        oap, self.ar(SSB + (h // 2) * 64, 64)[hp:hp + 64, :],
                        self.ar(QFM + c * 128, 128)[hp:hp + 64, :],
                        start=False, stop=last))
                else:
                    psO_tok = k.sig("tensor", lambda e, h=h, oap=oap, first=first, last=last: e.matmul(
                        oap, self.ar(VTB + h * 64, 64), self.ar(EB + h * 128, 128),
                        start=first, stop=last))
                if h % 4 == 3:
                    bt.set_w(4 + h // 4, psO_tok)
            if _stop == "pso":
                continue
            for h in range(16):
                sbk = h // 8
                if h % 8 == 0:
                    bt.pre_mm(sbk)
                fn = (lambda e, h=h, sbk=sbk: e.matmul(
                    self.PS[0:64, sbk * 512 + (h % 8) * 64:sbk * 512 + (h % 8) * 64 + 64],
                    self.ar(KTB + h * 64, 64), self.ar(VTB + h * 64, 64),
                    start=(h % 8 == 0), stop=(h % 8 == 7)))
                ts_ = k.sig("tensor", fn)
                if h % 8 == 7:
                    bt.set_w(sbk, ts_)
            bt.pre_read(0, "vector"); bt.pre_read(1, "vector")
            k.wait_self("vector")
            tadd = k.sig("vector", lambda e, sstc=sstc, sstn=sstn: e.tensor_add(
                sstn[:, :], sstc[:, :], self.PS[0:64, 0:1024]))
            bt.add_r(0, tadd); bt.add_r(1, tadd)
            k.wait_self("vector")
            k.wait("vector", psO_tok)
            sv = sstn[:, :].rearrange("p (c two dd) -> p c two dd", two=2, dd=64)
            k.sig("vector", lambda e, sv=sv: e.tensor_copy(
                self.ar(SSB, 512)[0:64, :].rearrange("p (c dd) -> p c dd", dd=64),
                sv[:, :, 0, :]))
            tssb = k.sig("vector", lambda e, sv=sv: e.tensor_copy(
                self.ar(SSB, 512, parts=(64, 128)).rearrange("p (c dd) -> p c dd", dd=64),
                sv[:, :, 1, :]))
            if _stop == "state":
                continue
            self._l2_ktb_rd = None
            k.wait("vector", ts_)
            for half in range(2):
                bt.pre_read(4 + half * 2, "vector"); bt.pre_read(5 + half * 2, "vector")
                tv = k.sig("vector", lambda e, half=half: e.tensor_copy(
                    self.ar(OBT + half * 1024, 1024, parts=(0, 64)),
                    self.PS[0:64, (4 + half * 2) * 512:(4 + half * 2) * 512 + 1024]))
                bt.add_r(4 + half * 2, tv); bt.add_r(5 + half * 2, tv)
                self._l2_vtb_rd = tv
                self._l2_ktb_rd = tv
            if _stop == "evac":
                continue
            for h in range(16):
                bt.pre_mm(7)
                tt = k.sig("tensor", lambda e, h=h: e.transpose(
                    self.psb(7, 64, 0), self.ar(OBT + h * 128, 128, parts=(0, 64)),
                    self.IDB[0:64, 0:64]))
                bt.set_w(7, tt)
                bt.pre_read(7, "vector")
                tv = k.sig("vector", lambda e, h=h: e.tensor_copy(
                    self.SCRA[:, h * 64:(h + 1) * 64], self.psb(7, 64, 0)))
                bt.add_r(7, tv)
            if _stop == "trans":
                continue
            k.wait_self("vector")
            k.sig("vector", lambda e: e.tensor_mul(
                self.SCRB[:, 0:D], self.SCRA[:, 0:D], self.ar(GTF, 2048, F32)))
            if _stop == "gating":
                continue
            k.wait_self("vector")
            tl = self.ln_post(self.SCRB[:, 0:D], t, dst_ap=self.SCRA[:, 0:D])
            k.wait_self("vector")
            k.sig("vector", lambda e: e.tensor_mul(self.SCRC[:, 0:D], self.SCRA[:, 0:D], GAM))
            k.wait_self("vector")
            k.sig("vector", lambda e: e.scalar_tensor_tensor(
                self.SCRA[:, 0:D], self.SCRC[:, 0:D], 1.0, BET, ALU.mult, ALU.add))
            if _stop == "ln1":
                continue
            k.wait_self("vector")
            tcb = k.sig("vector", lambda e: e.tensor_copy(self.ar(OB, 1024), self.SCRA[:, 0:D]))
            k.wait("tensor", tcb)
            self.tm2fm(lambda c: self.ar(OB + c * 128, 128),
                       lambda g0, n: self.ar(XFM + g0 * 128, n * 128), 8)
            for s2 in range(2):
                bk = 2 + s2
                bt.pre_mm(bk)
                tm = None
                for c in range(8):
                    fn = (lambda e, c=c, s2=s2, bk=bk: e.matmul(
                        self.ps(bk), self.ar(XFM + c * 128, 128),
                        self.ar(WOo + c * 1024 + s2 * 512, 512),
                        start=(c == 0), stop=(c == 7)))
                    tm = k.sig("tensor", fn) if c == 7 else k.emit("tensor", fn)
                bt.set_w(bk, tm)
            k.wait("vector", getattr(self, "_ln_src_rd", None))
            for s2 in range(2):
                bt.pre_read(2 + s2, "vector")
                tr = k.sig("vector", lambda e, s2=s2, t=t: e.tensor_add(
                    self.SCRB[:, s2 * 512:(s2 + 1) * 512], self.ps(2 + s2),
                    self.X[:, t * D + s2 * 512:t * D + (s2 + 1) * 512]))
                bt.add_r(2 + s2, tr)
            k.wait("scalar", tr)
            tln = self.ln_post(self.SCRB[:, 0:D], t)
            k.wait("tensor", tln)
            self.stat_reset()
            if t < NT - 1:
                k.wait("sync", k.tok("vector"))
                k.dma("sync", self.ar(COS, 1024), d["l2cos"][(t + 1) * 128:(t + 2) * 128, :])
                k.dma("sync", self.ar(SINA, 1024), d["l2sina"][(t + 1) * 128:(t + 2) * 128, :])

    def emit_l2f(self):
        k, bt, d = self.k, self.bt, self.dram
        CW, SW, F32A, XB, XFM = 0, 8192, 16384, 24576, 25600
        PI2 = self.STAT[:, 2:3]
        ZZ = self.STAT[:, 3:4]
        k.sig("vector", lambda e: e.memset(PI2, 1.5707963267948966))
        k.sig("vector", lambda e: e.memset(ZZ, 0.0))
        k.barrier()
        dt = [None, None]
        for wi, (nm1, nm2, dst) in enumerate((("l2_theta", "l2_theta_noise", CW),
                                              ("l2_phi", "l2_phi_noise", SW))):
            for ch in range(8):
                sl = F32A + (ch % 2) * 4096
                if dt[ch % 2] is not None:
                    k.wait("sync", dt[ch % 2])
                k.dma("sync", self.ar(sl, 2048, F32), d[nm1][ch * 128:(ch + 1) * 128, :])
                tw = k.dma("sync", self.ar(sl + 2048, 2048, F32), d[nm2][ch * 128:(ch + 1) * 128, :])
                k.wait("vector", tw)
                k.wait_self("vector")
                ta = k.sig("vector", lambda e, sl=sl: e.tensor_add(
                    self.SCRC[:, 0:D], self.ar(sl, 2048, F32), self.ar(sl + 2048, 2048, F32)))
                dt[ch % 2] = ta
                k.wait("scalar", ta)
                bias = PI2 if wi == 0 else ZZ
                ts_ = k.sig("scalar", lambda e, ch=ch, dst=dst, bias=bias: e.activation(
                    self.ar(dst + ch * 1024, 1024), self.SCRC[:, 0:D], AF.Sin, bias=bias))
                k.wait("vector", ts_)
        k.barrier()
        for t in range(NT):
            k.wait("vector", k.tok("tensor")); k.wait("vector", k.tok("scalar"))
            tc = k.sig("vector", lambda e, t=t: e.tensor_copy(
                self.ar(XB, 1024), self.X[:, t * D:(t + 1) * D]))
            k.wait("tensor", tc)
            self.tm2fm(lambda c: self.ar(XB + c * 128, 128),
                       lambda g0, n: self.ar(XFM + g0 * 128, n * 128), 8)
            for wi, woff in ((0, CW), (1, SW)):
                for s2 in range(2):
                    bk = wi * 2 + s2
                    bt.pre_mm(bk)
                    tm = None
                    for c in range(8):
                        fn = (lambda e, c=c, s2=s2, woff=woff, bk=bk: e.matmul(
                            self.ps(bk), self.ar(XFM + c * 128, 128),
                            self.ar(woff + c * 1024 + s2 * 512, 512),
                            start=(c == 0), stop=(c == 7)))
                        tm = k.sig("tensor", fn) if c == 7 else k.emit("tensor", fn)
                    bt.set_w(bk, tm)
            for s2 in range(2):
                bt.pre_read(s2, "scalar")
                te = k.sig("scalar", lambda e, s2=s2: e.activation(
                    self.SCRB[:, s2 * 512:(s2 + 1) * 512], self.ps(s2), AF.Square))
                bt.add_r(s2, te)
                bt.pre_read(2 + s2, "scalar")
                tv = k.sig("scalar", lambda e, s2=s2: e.activation(
                    self.SCRC[:, s2 * 512:(s2 + 1) * 512], self.ps(2 + s2), AF.Square))
                bt.add_r(2 + s2, tv)
            k.wait("vector", tv)
            k.wait_self("vector")
            tm2 = k.sig("vector", lambda e: e.tensor_add(
                self.SCRD[:, 0:512], self.SCRB[:, 0:512], self.SCRC[:, 0:512]))
            tm3 = k.sig("vector", lambda e: e.tensor_add(
                self.SCRA[:, 512:1024], self.SCRB[:, 512:1024], self.SCRC[:, 512:1024]))
            k.wait("scalar", tm2); k.wait("scalar", tm3)
            ts1 = k.sig("scalar", lambda e: e.activation(
                self.SCRB[:, 0:512], self.SCRD[:, 0:512], AF.Sqrt))
            ts2 = k.sig("scalar", lambda e: e.activation(
                self.SCRB[:, 512:1024], self.SCRA[:, 512:1024], AF.Sqrt))
            k.wait("vector", ts2)
            k.wait("vector", getattr(self, "_ln_src_rd", None))
            k.wait_self("vector")
            tr = k.sig("vector", lambda e, t=t: e.tensor_add(
                self.SCRA[:, 0:D], self.SCRB[:, 0:D], self.X[:, t * D:(t + 1) * D]))
            k.wait("scalar", tr)
            tln = self.ln_post(self.SCRA[:, 0:D], t)
            k.wait("tensor", tln)
            self.stat_reset()


    def emit_l3a(self):
        k, bt, d = self.k, self.bt, self.dram
        KFM, VT, QFM, PT = 0, 16384, 32768, 36864
        AVF = QFM
        WBIG, XB, XFM = 45056, 53248, 54272
        ZRa = self.ROWS[32:33, 0:512]
        ZRb = self.ROWS[64:65, 0:512]
        KM = self.ROWS[0:1, 24:25]
        k.sig("vector", lambda e: e.memset(KM, 0.0))
        k.sig("vector", lambda e: e.memset(self.ROWS[0:32, 20:21], 0.0))
        for bi in range(1, 5):
            k.dma("gpsimd", self.ROWSB[64:65, (bi - 1) * 1024:bi * 1024], d[f"l3_b{bi}"][:, :])
        k.barrier()

        def load_w(nm, tok):
            if tok is not None:
                k.wait("gpsimd", tok)
            return k.dma("gpsimd", self.ar(WBIG, 8192).rearrange("p (c n) -> p c n", n=1024),
                         d[nm].rearrange("(c p) m -> p c m", p=128))

        def xfm_tile(t):
            k.wait("vector", k.tok("tensor")); k.wait("vector", k.tok("scalar"))
            tc = k.sig("vector", lambda e, t=t: e.tensor_copy(
                self.ar(XB, 1024), self.X[:, t * D:(t + 1) * D]))
            k.wait("tensor", tc)
            self.tm2fm(lambda c: self.ar(XB + c * 128, 128),
                       lambda g0, n: self.ar(XFM + g0 * 128, n * 128), 8)

        def proj(bi, scale, tok):
            for s2 in range(2):
                bk = s2
                bt.pre_mm(bk)
                k.wait("tensor", tok)
                for c in range(8):
                    k.emit("tensor", lambda e, c=c, s2=s2, bk=bk: e.matmul(
                        self.ps(bk), self.ar(XFM + c * 128, 128),
                        self.ar(WBIG + c * 1024 + s2 * 512, 512),
                        start=(c == 0), stop=False))
                tq = k.sig("tensor", lambda e, s2=s2, bi=bi, bk=bk: e.matmul(
                    self.ps(bk), self.ONB[64:65, :],
                    self.ROWSB[64:65, (bi - 1) * 1024 + s2 * 512:(bi - 1) * 1024 + (s2 + 1) * 512],
                    start=False, stop=True))
                bt.set_w(bk, tq)
            for s2 in range(2):
                bt.pre_read(s2, "scalar")
                te = k.sig("scalar", lambda e, s2=s2, scale=scale: e.activation(
                    self.SCRB[:, s2 * 512:(s2 + 1) * 512], self.ps(s2), AF.Copy, scale=scale))
                bt.add_r(s2, te)
            return te

        wtok = load_w("l3_w2", None)
        for t in range(NT):
            xfm_tile(t)
            te = proj(2, 1.0, wtok)
            kn = self.STAT[:, 8:9]
            k.wait("scalar", te)
            t2 = k.sig("scalar", lambda e: e.activation(self.SCRC[:, 0:D], self.SCRB[:, 0:D], AF.Square))
            k.wait("vector", t2)
            k.wait_self("vector")
            t3 = k.sig("vector", lambda e, kn=kn: e.reduce_sum(kn, self.SCRC[:, 0:D], axis=AX.X))
            k.wait("scalar", t3)
            t4 = k.sig("scalar", lambda e, kn=kn: e.activation(kn, kn, AF.Sqrt))
            k.wait("vector", t4)
            k.wait_self("vector")
            k.sig("vector", lambda e, kn=kn: e.tensor_copy(self.SCRD[0:64, 0:1], kn[64:128, :]))
            k.wait_self("vector")
            k.sig("vector", lambda e, kn=kn: e.tensor_tensor(
                self.SCRD[0:64, 1:2], kn[0:64, :], self.SCRD[0:64, 0:1], ALU.max))
            k.wait_self("vector")
            k.sig("vector", lambda e: e.tensor_copy(self.SCRD[0:32, 2:3], self.SCRD[32:64, 1:2]))
            k.wait_self("vector")
            tf = k.sig("vector", lambda e: e.tensor_tensor(
                self.ROWS[0:32, 20:21], self.SCRD[0:32, 1:2], self.SCRD[0:32, 2:3], ALU.max))
            bt.pre_mm(6)
            k.wait("tensor", tf)
            tt = k.sig("tensor", lambda e: e.transpose(
                self.PS[0:1, 6 * 512:6 * 512 + 32], self.ROWS[0:32, 20:21], self.IDF[0:32, 0:32]))
            bt.set_w(6, tt)
            bt.pre_read(6, "vector")
            k.wait_self("vector")
            tr = k.sig("vector", lambda e: e.reduce_max(
                self.ROWS[0:1, 26:27], self.PS[0:1, 6 * 512:6 * 512 + 32], axis=AX.X))
            bt.add_r(6, tr)
            k.wait_self("vector")
            k.sig("vector", lambda e: e.scalar_tensor_tensor(
                self.ROWS[0:1, 27:28], KM, 1.0, self.ROWS[0:1, 26:27], ALU.mult, ALU.max))
            k.wait_self("vector")
            k.sig("vector", lambda e: e.tensor_copy(KM, self.ROWS[0:1, 27:28]))
            k.wait_self("vector")
            tb = k.sig("vector", lambda e: e.tensor_copy(self.ar(XB, 1024), self.SCRB[:, 0:D]))
            k.wait("tensor", tb)
            for c in range(8):
                bt.pre_mm(7)
                tt2 = k.sig("tensor", lambda e, c=c: e.transpose(
                    self.psb(7, 128, 0), self.ar(XB + c * 128, 128), self.IDB[:, :]))
                bt.set_w(7, tt2)
                bt.pre_read(7, "vector")
                tv = k.sig("vector", lambda e, c=c, t=t: e.tensor_copy(
                    self.ar(KFM + c * 2048 + t * 128, 128), self.psb(7, 128, 0)))
                bt.add_r(7, tv)
        wtok = load_w("l3_w3", k.tok("tensor"))
        for t in range(NT):
            xfm_tile(t)
            for s2 in range(2):
                bk = s2
                bt.pre_mm(bk)
                k.wait("tensor", wtok)
                for c in range(8):
                    k.emit("tensor", lambda e, c=c, s2=s2, bk=bk: e.matmul(
                        self.ps(bk), self.ar(XFM + c * 128, 128),
                        self.ar(WBIG + c * 1024 + s2 * 512, 512),
                        start=(c == 0), stop=False))
                tq = k.sig("tensor", lambda e, s2=s2, bk=bk: e.matmul(
                    self.ps(bk), self.ONB[64:65, :],
                    self.ROWSB[64:65, 2 * 1024 + s2 * 512:2 * 1024 + (s2 + 1) * 512],
                    start=False, stop=True))
                bt.set_w(bk, tq)
                bt.pre_read(bk, "vector")
                tv = k.sig("vector", lambda e, s2=s2, t=t, bk=bk: e.tensor_copy(
                    self.ar(VT + t * 1024 + s2 * 512, 512), self.ps(bk)))
                bt.add_r(bk, tv)
        for g in range(4):
            tiles = [4 * g + j for j in range(4)]
            wtok = load_w("l3_w1", k.tok("tensor"))
            for j, t in enumerate(tiles):
                xfm_tile(t)
                te = proj(1, 1.0 / 32.0, wtok)
                qn = self.STAT[:, 8:9]
                k.wait("scalar", te)
                t2 = k.sig("scalar", lambda e: e.activation(self.SCRC[:, 0:D], self.SCRB[:, 0:D], AF.Square))
                k.wait("vector", t2)
                k.wait_self("vector")
                t3 = k.sig("vector", lambda e, qn=qn: e.reduce_sum(qn, self.SCRC[:, 0:D], axis=AX.X))
                k.wait("scalar", t3)
                t4 = k.sig("scalar", lambda e, qn=qn: e.activation(qn, qn, AF.Sqrt))
                self.bcast_row(6, KM, 1, pb=0)
                bt.pre_read(6, "vector")
                k.wait("vector", t4)
                k.wait_self("vector")
                tb = k.sig("vector", lambda e, qn=qn: e.scalar_tensor_tensor(
                    self.SCRD[:, 0:1], qn, -1.0, self.PS[:, 6 * 512:6 * 512 + 1], ALU.mult, ALU.mult))
                bt.add_r(6, tb)
                bt.pre_mm(6)
                k.wait("tensor", tb)
                tt = k.sig("tensor", lambda e: e.transpose(
                    self.PS[0:1, 6 * 512 + 128:6 * 512 + 256], self.SCRD[:, 0:1], self.IDF[:, :]))
                bt.set_w(6, tt)
                bt.pre_read(6, "vector")
                k.wait("vector", getattr(self, "_l3_bias_rd", None))
                tv = k.sig("vector", lambda e, j=j: e.tensor_copy(
                    self.ROWSB[0:1, j * 128:j * 128 + 128],
                    self.PS[0:1, 6 * 512 + 128:6 * 512 + 256]))
                bt.add_r(6, tv)
                k.wait_self("vector")
                tbq = k.sig("vector", lambda e: e.tensor_copy(self.ar(XB, 1024), self.SCRB[:, 0:D]))
                k.wait("tensor", tbq)
                for c in range(8):
                    bt.pre_mm(7)
                    tt2 = k.sig("tensor", lambda e, c=c: e.transpose(
                        self.psb(7, 128, 0), self.ar(XB + c * 128, 128), self.IDB[:, :]))
                    bt.set_w(7, tt2)
                    bt.pre_read(7, "vector")
                    tv = k.sig("vector", lambda e, c=c, j=j: e.tensor_copy(
                        self.ar(QFM + c * 512 + j * 128, 128), self.psb(7, 128, 0)))
                    bt.add_r(7, tv)
            k.wait("tensor", k.tok("vector")); k.wait("tensor", k.tok("scalar"))
            for kc in range(16):
                sbk = kc % 2
                bt.pre_mm(sbk)
                for c in range(8):
                    k.emit("tensor", lambda e, c=c, kc=kc, sbk=sbk: e.matmul(
                        self.ps(sbk), self.ar(KFM + c * 2048 + kc * 128, 128),
                        self.ar(QFM + c * 512, 512), start=(c == 0), stop=False))
                tsc = k.sig("tensor", lambda e, sbk=sbk: e.matmul(
                    self.ps(sbk), self.ONB[0:1, :], self.ROWSB[0:1, 0:512],
                    start=False, stop=True))
                bt.set_w(sbk, tsc)
                self._l3_bias_rd = tsc
                bt.pre_read(sbk, "scalar")
                texp = k.sig("scalar", lambda e, kc=kc, sbk=sbk: e.activation(
                    self.ar(PT + kc * 512, 512), self.ps(sbk), AF.Exp))
                bt.add_r(sbk, texp)
            k.wait("tensor", texp)
            bt.pre_mm(2)
            for kc in range(16):
                fn = (lambda e, kc=kc: e.matmul(
                    self.PS[0:1, 2 * 512:2 * 512 + 512], self.ONB[:, 0:1],
                    self.ar(PT + kc * 512, 512), start=(kc == 0), stop=(kc == 15)))
                tz = k.sig("tensor", fn) if kc == 15 else k.emit("tensor", fn)
            bt.set_w(2, tz)
            bt.pre_read(2, "vector")
            tzc = k.sig("vector", lambda e: e.tensor_copy(ZRa, self.PS[0:1, 2 * 512:2 * 512 + 512]))
            bt.add_r(2, tzc)
            k.wait_self("vector")
            k.sig("vector", lambda e: e.reciprocal(ZRb, ZRa))
            k.wait_self("vector")
            self.bcast_row(3, ZRb, 512, pb=64)
            bt.pre_read(3, "vector")
            tzr = k.sig("vector", lambda e: e.tensor_copy(self.SCRD[:, 0:512], self.PS[:, 3 * 512:3 * 512 + 512]))
            bt.add_r(3, tzr)
            for ec in range(8):
                ab = 4 + ec % 2
                bt.pre_mm(ab)
                for kc in range(16):
                    fn = (lambda e, ec=ec, kc=kc, ab=ab: e.matmul(
                        self.ps(ab), self.ar(VT + kc * 1024 + ec * 128, 128),
                        self.ar(PT + kc * 512, 512), start=(kc == 0), stop=(kc == 15)))
                    tav = k.sig("tensor", fn) if kc == 15 else k.emit("tensor", fn)
                bt.set_w(ab, tav)
                bt.pre_read(ab, "vector")
                k.wait_self("vector")
                tev = k.sig("vector", lambda e, ec=ec, ab=ab: e.tensor_tensor(
                    self.ar(AVF + ec * 512, 512), self.ps(ab), self.SCRD[:, 0:512], ALU.mult))
                bt.add_r(ab, tev)
            wtok = load_w("l3_w4", k.tok("tensor"))
            k.wait("tensor", k.tok("vector"))
            for j, t in enumerate(tiles):
                for s2 in range(2):
                    bk = s2
                    bt.pre_mm(bk)
                    k.wait("tensor", wtok)
                    for c in range(8):
                        k.emit("tensor", lambda e, c=c, j=j, s2=s2, bk=bk: e.matmul(
                            self.ps(bk), self.ar(AVF + c * 512 + j * 128, 128),
                            self.ar(WBIG + c * 1024 + s2 * 512, 512),
                            start=(c == 0), stop=False))
                    tq = k.sig("tensor", lambda e, s2=s2, bk=bk: e.matmul(
                        self.ps(bk), self.ONB[64:65, :],
                        self.ROWSB[64:65, 3 * 1024 + s2 * 512:3 * 1024 + (s2 + 1) * 512],
                        start=False, stop=True))
                    bt.set_w(bk, tq)
                k.wait("vector", getattr(self, "_ln_src_rd", None))
                for s2 in range(2):
                    bt.pre_read(s2, "vector")
                    tr = k.sig("vector", lambda e, s2=s2, t=t: e.tensor_add(
                        self.SCRA[:, s2 * 512:(s2 + 1) * 512], self.ps(s2),
                        self.X[:, t * D + s2 * 512:t * D + (s2 + 1) * 512]))
                    bt.add_r(s2, tr)
                k.wait("scalar", tr)
                tln = self.ln_post(self.SCRA[:, 0:D], t)
                k.wait("tensor", tln)
                self.stat_reset()
            if g < 3:
                wtok = None

    def emit_l3f(self):
        k, bt, d = self.k, self.bt, self.dram
        HFM, XQF, WGS, WUS, WDS, XB = 0, 12288, 16384, 18432, 20480, 21504
        for tg in range(4):
            tiles = [tg * 4 + j for j in range(4)]
            for j, t in enumerate(tiles):
                k.wait("vector", k.tok("tensor")); k.wait("vector", k.tok("scalar"))
                tc = k.sig("vector", lambda e, t=t: e.tensor_copy(
                    self.ar(XB, 1024), self.X[:, t * D:(t + 1) * D]))
                k.wait("tensor", tc)
                for c in range(8):
                    bt.pre_mm(7)
                    tt = k.sig("tensor", lambda e, c=c: e.transpose(
                        self.psb(7, 128, 0), self.ar(XB + c * 128, 128), self.IDB[:, :]))
                    bt.set_w(7, tt)
                    bt.pre_read(7, "vector")
                    tv = k.sig("vector", lambda e, c=c, j=j: e.tensor_copy(
                        self.ar(XQF + c * 512 + j * 128, 128), self.psb(7, 128, 0)))
                    bt.add_r(7, tv)
            k.wait("tensor", k.tok("vector"))
            gt = [None, None]
            for hc in range(24):
                gsl = WGS + (hc % 2) * 1024
                usl = WUS + (hc % 2) * 1024
                if gt[hc % 2] is not None:
                    k.wait("gpsimd", gt[hc % 2])
                k.dma("gpsimd", self.ar(gsl, 1024).rearrange("p (c n) -> p c n", n=128),
                      d["l3_wgt"].rearrange("(c p) h -> p c h", p=128)[:, :, hc * 128:(hc + 1) * 128])
                tw = k.dma("gpsimd", self.ar(usl, 1024).rearrange("p (c n) -> p c n", n=128),
                           d["l3_wut"].rearrange("(c p) h -> p c h", p=128)[:, :, hc * 128:(hc + 1) * 128])
                for wi, sl in ((0, gsl), (1, usl)):
                    bk = wi
                    bt.pre_mm(bk)
                    k.wait("tensor", tw)
                    tm = None
                    for c in range(8):
                        fn = (lambda e, c=c, sl=sl, bk=bk: e.matmul(
                            self.ps(bk), self.ar(sl + c * 128, 128),
                            self.ar(XQF + c * 512, 512), start=(c == 0), stop=(c == 7)))
                        tm = k.sig("tensor", fn) if c == 7 else k.emit("tensor", fn)
                    bt.set_w(bk, tm)
                    gt[hc % 2] = tm
                bt.pre_read(0, "scalar")
                te = k.sig("scalar", lambda e: e.activation(
                    self.SCRB[:, 0:512], self.ps(0), AF.Silu, scale=1.0 / 127.0))
                bt.add_r(0, te)
                bt.pre_read(1, "vector")
                k.wait("vector", te)
                k.wait_self("vector")
                tv = k.sig("vector", lambda e, hc=hc: e.scalar_tensor_tensor(
                    self.ar(HFM + hc * 512, 512), self.ps(1), 1.0 / 127.0,
                    self.SCRB[:, 0:512], ALU.mult, ALU.mult))
                bt.add_r(1, tv)
            k.wait("tensor", k.tok("vector"))
            dt = [None, None]
            for hc in range(24):
                for s2 in range(2):
                    sl = WDS + s2 * 512
                    if dt[s2] is not None:
                        k.wait("gpsimd", dt[s2])
                    tw = k.dma("gpsimd", self.ar(sl, 512),
                               d["l3_wdt"][hc * 128:(hc + 1) * 128, s2 * 512:(s2 + 1) * 512])
                    if hc == 0:
                        for b in range(s2 * 4, s2 * 4 + 4):
                            bt.pre_mm(b)
                    k.wait("tensor", tw)
                    tm = None
                    for tc2 in range(4):
                        bk = s2 * 4 + tc2
                        fn = (lambda e, hc=hc, tc2=tc2, sl=sl, bk=bk: e.matmul(
                            self.ps(bk), self.ar(HFM + hc * 512 + tc2 * 128, 128),
                            self.ar(sl, 512), start=(hc == 0), stop=(hc == 23)))
                        tm = k.sig("tensor", fn) if tc2 == 3 else k.emit("tensor", fn)
                    dt[s2] = tm
                    if hc == 23:
                        for b in range(s2 * 4, s2 * 4 + 4):
                            bt.set_w(b, tm)
            for tc2 in range(4):
                t = tg * 4 + tc2
                k.wait("vector", getattr(self, "_ln_src_rd", None))
                for s2 in range(2):
                    bk = s2 * 4 + tc2
                    bt.pre_read(bk, "vector")
                    tr = k.sig("vector", lambda e, bk=bk, s2=s2, t=t: e.scalar_tensor_tensor(
                        self.SCRA[:, s2 * 512:(s2 + 1) * 512], self.ps(bk), 1.0 / 127.0,
                        self.X[:, t * D + s2 * 512:t * D + (s2 + 1) * 512], ALU.mult, ALU.add))
                    bt.add_r(bk, tr)
                k.wait("scalar", tr)
                tln = self.ln_post(self.SCRA[:, 0:D], t)
                k.wait("tensor", tln)
                self.stat_reset()


def build_program(blocks=ALL_BLOCKS):
    mk = MK(blocks)
    return mk.build()


def kernel(**inputs):
    shared = _prep_weights(inputs)
    shared.update(_consts())
    nc = build_program()
    xfull = np.asarray(inputs["L0_attn_x"], dtype=np.float32)
    in_maps = []
    for core in range(8):
        m = dict(shared)
        m["x"] = np.ascontiguousarray(xfull[core])
        in_maps.append(m)
    res = run_bass_kernel_spmd(nc, in_maps, core_ids=list(range(8)))
    return np.stack([np.asarray(r["out"]) for r in res.results], axis=0).astype(np.float32)
